# version 06
# baseline (speedup 1.0000x reference)
import numpy as np
import contextlib
import concourse.bass as bass
import concourse.mybir as mybir
from concourse.bass_utils import run_bass_kernel_spmd

F32 = mybir.dt.float32
BF16 = mybir.dt.bfloat16
I32 = mybir.dt.int32
AF = mybir.ActivationFunctionType
ALU = mybir.AluOpType
AX = mybir.AxisListType

P = 128
S = 4096
E = 1024
IN = 2048
NH = 4
DH = 512
HID = 3072
KCV = 4
SC = float(DH) ** -0.5
DEQ = 1.0 / 127.0

ENGS = ["sync", "tensor", "vector", "scalar", "gpsimd"]
NQ = 4


class Emitter:

    def __init__(self, nc):
        self.nc = nc
        self.ops = {e: [] for e in ENGS}
        self.cnt = {}
        self.ndma = {e: 0 for e in ENGS}
        self.semnames = []
        for e in ENGS:
            self._reg(f"c_{e}")
            for i in range(NQ):
                self._reg(f"q_{e}{i}")
        self.sems = {}
        self.writers = {}
        self.readers = {}
        self.fence = None

    def _reg(self, name):
        self.semnames.append(name)
        self.cnt[name] = 0

    def make_sems(self, stack):
        for n in self.semnames:
            self.sems[n] = stack.enter_context(self.nc.semaphore(n))

    def _emit(self, eng, fn, r, w, extra, dma, self_inc):
        deps = []
        for nm in r:
            deps.extend(self.writers.get(nm, ()))
        for nm in w:
            rd = self.readers.get(nm, ())
            deps.extend(rd)
            if rd:
                self.writers[nm] = []
                self.readers[nm] = []
        deps.extend(extra)
        if self.fence is not None:
            deps.append(self.fence)
        if dma:
            k = self.ndma[eng]
            self.ndma[eng] += 1
            sem = f"q_{eng}{k % NQ}"
            deps.append((sem, self.cnt[sem]))
            amt = 16
        else:
            sem = f"c_{eng}"
            amt = 1
        self.ops[eng].append((deps, fn, sem, amt, self_inc, eng))
        self.cnt[sem] += amt
        tok = (sem, self.cnt[sem])
        for nm in r:
            self.readers.setdefault(nm, []).append(tok)
        for nm in w:
            self.writers.setdefault(nm, []).append(tok)
        return tok

    def op(self, eng, fn, r=(), w=(), extra=(), self_inc=False):
        return self._emit(eng, fn, r, w, list(extra), False, self_inc)

    def dma(self, eng, out_ap, in_ap, r=(), w=(), extra=(), **kw):
        return self._emit(eng, lambda EE: EE.dma_start(out_ap, in_ap, **kw),
                          r, w, list(extra), True, False)

    def dma_op(self, eng, fn, r=(), w=(), extra=(), self_inc=False):
        return self._emit(eng, fn, r, w, list(extra), True, self_inc)

    def barrier(self):
        marks = [(f"c_{e}", self.cnt[f"c_{e}"] + 1) for e in ENGS]
        qmarks = [(f"q_{e}{i}", self.cnt[f"q_{e}{i}"]) for e in ENGS for i in range(NQ)]
        def mk(e):
            return lambda EE: EE.drain()
        for e in ENGS:
            self.ops[e].append(([], mk(e), f"c_{e}", 1, False, e))
            self.cnt[f"c_{e}"] += 1
        waits = marks + [q for q in qmarks if q[1] > 0]
        toks = []
        for e in ENGS:
            self.ops[e].append((list(waits), mk(e), f"c_{e}", 1, False, e))
            self.cnt[f"c_{e}"] += 1
            toks.append((f"c_{e}", self.cnt[f"c_{e}"]))
        self.writers = {}
        self.readers = {}
        self.fence = toks[0]
        return toks[0]

    def replay(self, block):
        for e in ENGS:
            def make(e):
                def body(EE):
                    waited = {}
                    for (waits, fn, sem, amt, self_inc, eng) in self.ops[e]:
                        coll = {}
                        for (wn, wv) in waits:
                            if wv > coll.get(wn, 0):
                                coll[wn] = wv
                        own = f"c_{e}"
                        for wn, wv in coll.items():
                            if wn == "c_tensor" and e == "tensor":
                                continue
                            if wn != own and waited.get(wn, -1) >= wv:
                                continue
                            EE.wait_ge(self.sems[wn], wv)
                            waited[wn] = wv
                        if self_inc:
                            fn(EE, self.sems[sem], amt)
                        else:
                            fn(EE).then_inc(self.sems[sem], amt)
                return body
            getattr(block, e)(make(e))


def build_program():
    nc = bass.Bass("TRN2", target_bir_lowering=False)
    dr = lambda name, shape, dt, kind="ExternalInput": nc.dram_tensor(name, shape, dt, kind=kind).ap()

    x = dr("x", [S, E], F32)
    wup = dr("wup", [E, 2 * IN], F32)
    convdiag = dr("convdiag", [16, KCV, P, P], F32)
    convb2d = dr("convb2d", [P, 16], F32)
    bdq = dr("bdq", [16, P, P], F32)
    bdk = dr("bdk", [16, P, P], F32)
    bdv = dr("bdv", [16, P, P], F32)
    bdqt = dr("bdqt", [16, P, P], F32)
    bdkt = dr("bdkt", [16, P, P], F32)
    bdvt = dr("bdvt", [16, P, P], F32)
    wigfg = dr("wigfg", [3 * IN, 8], F32)
    bigfg = dr("bigfg", [8, 1], F32)
    nw2d = dr("nw2d", [P, 16], F32)
    nwrow = dr("nwrow", [1, IN], F32)
    skip2d = dr("skip2d", [P, 16], F32)
    wdn = dr("wdn", [IN, E], F32)
    wgT = dr("wgT", [E, HID], I32)
    wuT = dr("wuT", [E, HID], I32)
    wdT = dr("wdT", [HID, E], I32)
    masks = dr("masks", [4, P, 512], F32)
    ident = dr("ident", [P, P], F32)
    onesrow = dr("onesrow", [1, P], F32)
    onescol = dr("onescol", [P, 1], F32)
    epsb = dr("epsb", [P, 4], F32)

    y = dr("y", [S // 2, E], F32, kind="ExternalOutput")

    s_lnx = dr("s_lnx", [S, E], BF16, kind="Internal")
    s_kT = dr("s_kT", [IN, S], BF16, kind="Internal")
    s_qT = dr("s_qT", [IN, S], BF16, kind="Internal")
    s_vt = dr("s_vt", [S, IN], BF16, kind="Internal")
    s_xact = dr("s_xact", [IN, S], BF16, kind="Internal")
    s_swz = dr("s_swz", [IN, S], BF16, kind="Internal")
    s_hst = dr("s_hst", [IN, S], BF16, kind="Internal")
    s_y1 = dr("s_y1", [S, E], F32, kind="Internal")
    s_y1own = dr("s_y1own", [S // 2, E], F32, kind="Internal")
    s_ln2 = dr("s_ln2", [S // 2, E], BF16, kind="Internal")
    s_gat = dr("s_gat", [3, NH, S], F32, kind="Internal")
    s_wg16 = dr("s_wg16", [E, HID], BF16, kind="Internal")
    s_wu16 = dr("s_wu16", [E, HID], BF16, kind="Internal")
    s_wd16 = dr("s_wd16", [HID, E], BF16, kind="Internal")

    with contextlib.ExitStack() as top:
        em = Emitter(nc)
        em.make_sems(top)

        def SB(stack, name, shape, dt=F32, **kw):
            return stack.enter_context(nc.sbuf_tensor(name, shape, dt, **kw))

        def PS(stack, name, shape, dt=F32):
            return stack.enter_context(nc.psum_tensor(name, shape, dt))

        ones_r = SB(top, "ones_r", [1, P])
        ones_cf = SB(top, "ones_cf", [P, 1])
        ones_cb = SB(top, "ones_cb", [P, 1], BF16)
        ident_s = SB(top, "ident_s", [P, P])
        nw2d_s = SB(top, "nw2d_s", [P, 16])
        skip2d_s = SB(top, "skip2d_s", [P, 16])
        bigfg_s = SB(top, "bigfg_s", [8, 1])
        nwrow_s = SB(top, "nwrow_s", [1, IN])

        epsb_s = SB(top, "epsb_s", [P, 4])
        em.dma("sync", epsb_s[:], epsb, w=["epsb_s"])
        em.dma("sync", ones_r[:], onesrow, w=["ones_r"])
        em.dma("sync", ones_cf[:], onescol, w=["ones_cf"])
        em.dma("sync", ident_s[:], ident, w=["ident_s"])
        em.dma("sync", nw2d_s[:], nw2d, w=["nw2d_s"])
        em.dma("sync", skip2d_s[:], skip2d, w=["skip2d_s"])
        em.dma("sync", bigfg_s[:], bigfg, w=["bigfg_s"])
        em.dma("sync", nwrow_s[:], nwrow, w=["nwrow_s"])
        em.op("scalar", lambda EE: EE.activation(ones_cb[:], ones_cf[:], AF.Copy),
              r=["ones_cf"], w=["ones_cb"])
        gacc = [SB(top, f"gacc{j}", [8, S]) for j in range(2)]

        with contextlib.ExitStack() as ph02:
            convb_s = SB(ph02, "convb_s", [P, 16])
            cd_b = SB(ph02, "cd_b", [P, 16 * KCV * P], BF16)
            bdq_b = SB(ph02, "bdq_b", [P, 16 * P], BF16)
            bdk_b = SB(ph02, "bdk_b", [P, 16 * P], BF16)
            bdv_b = SB(ph02, "bdv_b", [P, 16 * P], BF16)
            cqk_b = SB(ph02, "cqk_b", [P, 16 * 8], BF16)
            cv_b = SB(ph02, "cv_b", [P, 16 * 8], BF16)

            em.dma("sync", convb_s[:], convb2d, w=["convb_s"])
            with contextlib.ExitStack() as ph0:
                cstg = SB(ph0, "cstg", [P, 4 * P])
                bstg = SB(ph0, "bstg", [P, 3 * P])
                wig_s = SB(ph0, "wig_s", [P, 48 * 8])
                btstg = SB(ph0, "btstg", [P, 3 * P])
                ps_cq = PS(ph0, "ps_cq", [P, 8])
                ps_cv = PS(ph0, "ps_cv", [P, 8])
                for c in range(16):
                    em.dma("sync", cstg[:].rearrange("p (i q) -> p i q", i=4), convdiag[c].rearrange("i p q -> p i q"),
                           w=["cstg"])
                    em.op("scalar", lambda EE, c=c: EE.activation(
                        cd_b[:, c * 4 * P:(c + 1) * 4 * P], cstg[:], AF.Copy),
                        r=["cstg"], w=["cd_b"])
                for c in range(16):
                    em.dma("sync", bstg[:, 0:P], bdq[c], w=["bstg0"], r=[])
                    em.dma("sync", bstg[:, P:2 * P], bdk[c], w=["bstg1"])
                    em.dma("sync", bstg[:, 2 * P:3 * P], bdv[c], w=["bstg2"])
                    em.op("scalar", lambda EE, c=c: EE.activation(
                        bdq_b[:, c * P:(c + 1) * P], bstg[:, 0:P], AF.Copy),
                        r=["bstg0"], w=["bdq_b"])
                    em.op("scalar", lambda EE, c=c: EE.activation(
                        bdk_b[:, c * P:(c + 1) * P], bstg[:, P:2 * P], AF.Copy),
                        r=["bstg1"], w=["bdk_b"])
                    em.op("scalar", lambda EE, c=c: EE.activation(
                        bdv_b[:, c * P:(c + 1) * P], bstg[:, 2 * P:3 * P], AF.Copy),
                        r=["bstg2"], w=["bdv_b"])
                em.dma("sync", wig_s[:].rearrange("p (t g) -> p t g", t=48), wigfg.rearrange("(t p) g -> p t g", p=P),
                       w=["wig_s"])
                for c in range(16):
                    em.dma("sync", btstg[:, 0:P], bdqt[c], w=["btstg0"])
                    em.dma("sync", btstg[:, P:2 * P], bdkt[c], w=["btstg1"])
                    em.dma("sync", btstg[:, 2 * P:3 * P], bdvt[c], w=["btstg2"])
                    em.op("tensor", lambda EE, c=c: EE.matmul(
                        ps_cq[:], btstg[:, 0:P], wig_s[:, 8 * c:8 * c + 8],
                        start=True, stop=False), r=["btstg0", "wig_s"], w=["ps_cq"])
                    em.op("tensor", lambda EE, c=c: EE.matmul(
                        ps_cq[:], btstg[:, P:2 * P], wig_s[:, 8 * (16 + c):8 * (16 + c) + 8],
                        start=False, stop=True), r=["btstg1", "wig_s"], w=["ps_cq"])
                    em.op("tensor", lambda EE, c=c: EE.matmul(
                        ps_cv[:], btstg[:, 2 * P:3 * P], wig_s[:, 8 * (32 + c):8 * (32 + c) + 8],
                        start=True, stop=True), r=["btstg2", "wig_s"], w=["ps_cv"])
                    em.op("vector", lambda EE, c=c: EE.tensor_copy(
                        cqk_b[:, 8 * c:8 * c + 8], ps_cq[:]), r=["ps_cq"], w=["cqk_b"])
                    em.op("vector", lambda EE, c=c: EE.tensor_copy(
                        cv_b[:, 8 * c:8 * c + 8], ps_cv[:]), r=["ps_cv"], w=["cv_b"])
                em.barrier()

            with contextlib.ExitStack() as ph1:
                xt_b = [SB(ph1, f"xt{i}", [P, E]) for i in range(2)]
                xsq_b = [SB(ph1, f"xsq{i}", [P, E]) for i in range(2)]
                lnx_b = [SB(ph1, f"lnxo{i}", [P, E], BF16) for i in range(2)]
                st_a = SB(ph1, "st_a", [P, 16])
                for t in range(32):
                    i = t % 2
                    em.dma("sync", xt_b[i][:], x[P * t:P * (t + 1), :], w=[f"xt{i}"])
                    em.op("vector", lambda EE, i=i: EE.reduce_sum(st_a[:, 8 * i + 0:8 * i + 1], xt_b[i][:], AX.X),
                          r=[f"xt{i}"], w=[f"sa{i}_0"])
                    em.op("scalar", lambda EE, i=i: EE.activation(xsq_b[i][:], xt_b[i][:], AF.Square),
                          r=[f"xt{i}"], w=[f"xsq{i}"])
                    em.op("vector", lambda EE, i=i: EE.reduce_sum(st_a[:, 8 * i + 1:8 * i + 2], xsq_b[i][:], AX.X),
                          r=[f"xsq{i}"], w=[f"sa{i}_1"])
                    em.op("vector", lambda EE, i=i: EE.tensor_scalar(st_a[:, 8 * i + 2:8 * i + 3], st_a[:, 8 * i + 0:8 * i + 1], 1.0 / E, None, ALU.mult),
                          r=[f"sa{i}_0"], w=[f"sa{i}_2"])
                    em.op("vector", lambda EE, i=i: EE.tensor_scalar(st_a[:, 8 * i + 3:8 * i + 4], st_a[:, 8 * i + 1:8 * i + 2], 1.0 / E, None, ALU.mult),
                          r=[f"sa{i}_1"], w=[f"sa{i}_3"])
                    em.op("vector", lambda EE, i=i: EE.tensor_tensor(st_a[:, 8 * i + 4:8 * i + 5], st_a[:, 8 * i + 2:8 * i + 3], st_a[:, 8 * i + 2:8 * i + 3], ALU.mult),
                          r=[f"sa{i}_2"], w=[f"sa{i}_4"])
                    em.op("vector", lambda EE, i=i: EE.tensor_tensor(st_a[:, 8 * i + 5:8 * i + 6], st_a[:, 8 * i + 3:8 * i + 4], st_a[:, 8 * i + 4:8 * i + 5], ALU.subtract),
                          r=[f"sa{i}_3", f"sa{i}_4"], w=[f"sa{i}_5"])
                    em.op("scalar", lambda EE, i=i: EE.activation(st_a[:, 8 * i + 6:8 * i + 7], st_a[:, 8 * i + 5:8 * i + 6], AF.Sqrt, bias=epsb_s[:, 0:1]),
                          r=[f"sa{i}_5", "epsb_s"], w=[f"sa{i}_6"])
                    em.op("vector", lambda EE, i=i: EE.reciprocal(st_a[:, 8 * i + 7:8 * i + 8], st_a[:, 8 * i + 6:8 * i + 7]),
                          r=[f"sa{i}_6"], w=[f"sa{i}_7"])
                    em.op("vector", lambda EE, i=i: EE.tensor_scalar(
                        lnx_b[i][:], xt_b[i][:], st_a[:, 8 * i + 2:8 * i + 3], st_a[:, 8 * i + 7:8 * i + 8], ALU.subtract, ALU.mult),
                        r=[f"xt{i}", f"sa{i}_2", f"sa{i}_7"], w=[f"lnxo{i}"])
                    em.dma("gpsimd", s_lnx[P * t:P * (t + 1), :], lnx_b[i][:],
                           r=[f"lnxo{i}"], w=["s_lnx"])
                em.barrier()

            lnxT = [SB(ph02, f"lnxT{c}", [P, S], BF16, is_transpose_dest=True) for c in range(8)]
            for c in range(8):
                em.dma_op("sync", lambda EE, c=c: EE.dma_start_transpose(
                    lnxT[c][:], s_lnx[:, P * c:P * (c + 1)]), r=["s_lnx"], w=[f"lnxT{c}"])

            with contextlib.ExitStack() as ph2:
                wf_b0 = [SB(ph2, f"wf0_{k}", [P, P]) for k in range(8)]
                wf_b = [wf_b0, wf_b0]
                wb_b = [[SB(ph2, f"wb{j}_{k}", [P, P], BF16) for k in range(8)] for j in range(2)]
                xmb_b = [SB(ph2, f"xmb{j}", [P, 4 + S], BF16) for j in range(2)]
                xab_b = [SB(ph2, f"xab{j}", [P, S], BF16) for j in range(2)]
                kstg = [SB(ph2, f"kstg{j}", [P, S], BF16) for j in range(2)]
                qstg = [SB(ph2, f"qstg{j}", [P, S], BF16) for j in range(2)]
                vstg0 = SB(ph2, "vstg0", [P, 512], BF16)
                vstg = [vstg0, vstg0]
                ps_w = [PS(ph2, f"ps_w{j}", [P, 512]) for j in range(2)]
                ps_c = [PS(ph2, f"ps_c{j}", [P, 512]) for j in range(2)]
                ps_k = [PS(ph2, f"ps_k{j}", [P, 512]) for j in range(2)]
                ps_g = PS(ph2, "ps_g", [8, 512])

                for c in range(32):
                    j = c % 2
                    mlstm = c < 16
                    for k in range(8):
                        em.dma("sync", wf_b[j][k][:], wup[P * k:P * (k + 1), P * c:P * (c + 1)],
                               w=[f"wf0_{k}"])
                    for k in range(8):
                        em.op("scalar", lambda EE, j=j, k=k: EE.activation(
                            wb_b[j][k][:], wf_b[j][k][:], AF.Copy),
                            r=[f"wf0_{k}"], w=[f"wb{j}_{k}"])
                    if mlstm:
                        em.op("vector", lambda EE, j=j: EE.memset(xmb_b[j][:, 0:4], 0.0),
                              w=[f"xmbz{j}"], r=[f"xmb{j}"])
                    for t in range(8):
                        jj = t % 2
                        for k in range(8):
                            em.op("tensor", lambda EE, j=j, k=k, t=t, jj=jj: EE.matmul(
                                ps_w[jj][:], wb_b[j][k][:], lnxT[k][:, 512 * t:512 * (t + 1)],
                                start=(k == 0), stop=(k == 7)),
                                r=[f"wb{j}_{k}", f"lnxT{k}"], w=[f"ps_w{jj}"])
                        if mlstm:
                            em.op("scalar", lambda EE, j=j, t=t, jj=jj: EE.activation(
                                xmb_b[j][:, 4 + 512 * t:4 + 512 * (t + 1)], ps_w[jj][:], AF.Copy),
                                r=[f"ps_w{jj}"], w=[f"xmb{j}"])
                        else:
                            em.op("scalar", lambda EE, j=j, t=t, jj=jj: EE.activation(
                                xmb_b[j][:, 4 + 512 * t:4 + 512 * (t + 1)], ps_w[jj][:], AF.Silu),
                                r=[f"ps_w{jj}"], w=[f"xmb{j}"])
                    if not mlstm:
                        em.dma("gpsimd", s_swz[P * (c - 16):P * (c - 15), :], xmb_b[j][:, 4:4 + S],
                               r=[f"xmb{j}"], w=["s_swz"])
                        continue

                    for t in range(8):
                        jj = t % 2
                        for i in range(KCV):
                            em.op("tensor", lambda EE, j=j, c=c, i=i, t=t, jj=jj: EE.matmul(
                                ps_c[jj][:], cd_b[:, (c * 4 + i) * P:(c * 4 + i + 1) * P],
                                xmb_b[j][:, 512 * t + 1 + i:512 * t + 1 + i + 512],
                                start=(i == 0), stop=(i == KCV - 1)),
                                r=["cd_b", f"xmb{j}", f"xmbz{j}"], w=[f"ps_c{jj}"])
                        em.op("scalar", lambda EE, j=j, c=c, t=t, jj=jj: EE.activation(
                            xab_b[j][:, 512 * t:512 * (t + 1)], ps_c[jj][:], AF.Silu,
                            bias=convb_s[:, c:c + 1]),
                            r=[f"ps_c{jj}", "convb_s"], w=[f"xab{j}"])
                    em.dma("gpsimd", s_xact[P * c:P * (c + 1), :], xab_b[j][:],
                           r=[f"xab{j}"], w=["s_xact"])

                    kb = kstg[j]
                    qb = qstg[j]
                    for t in range(8):
                        jj = t % 2
                        em.op("tensor", lambda EE, j=j, c=c, t=t, jj=jj: EE.matmul(
                            ps_k[jj][:], bdk_b[:, c * P:(c + 1) * P], xab_b[j][:, 512 * t:512 * (t + 1)],
                            start=True, stop=True),
                            r=["bdk_b", f"xab{j}"], w=[f"ps_k{jj}"])
                        em.op("vector", lambda EE, jj=jj, t=t, kb=kb, j=j: EE.tensor_copy(
                            kb[:, 512 * t:512 * (t + 1)], ps_k[jj][:]),
                            r=[f"ps_k{jj}"], w=[f"kstg{j}"])
                        em.op("tensor", lambda EE, j=j, c=c, t=t, jj=jj: EE.matmul(
                            ps_k[jj][:], bdq_b[:, c * P:(c + 1) * P], xab_b[j][:, 512 * t:512 * (t + 1)],
                            start=True, stop=True),
                            r=["bdq_b", f"xab{j}"], w=[f"ps_k{jj}"])
                        em.op("vector", lambda EE, jj=jj, t=t, qb=qb, j=j: EE.tensor_scalar(
                            qb[:, 512 * t:512 * (t + 1)], ps_k[jj][:], SC, None, ALU.mult),
                            r=[f"ps_k{jj}"], w=[f"qstg{j}"])
                    em.dma("gpsimd", s_kT[P * c:P * (c + 1), :], kb[:], r=[f"kstg{j}"], w=["s_kT"])
                    em.dma("gpsimd", s_qT[P * c:P * (c + 1), :], qb[:], r=[f"qstg{j}"], w=["s_qT"])

                    for g in range(8):
                        jj = g % 2
                        for qq in range(4):
                            tt = 4 * g + qq
                            em.op("tensor", lambda EE, j=j, c=c, tt=tt, jj=jj, qq=qq: EE.matmul(
                                ps_c[jj][:, P * qq:P * (qq + 1)],
                                xmb_b[j][:, 4 + P * tt:4 + P * (tt + 1)], bdv_b[:, c * P:(c + 1) * P],
                                start=True, stop=True),
                                r=["bdv_b", f"xmb{j}"], w=[f"ps_c{jj}"])
                        em.op("scalar", lambda EE, jj=jj: EE.activation(vstg[jj][:], ps_c[jj][:], AF.Copy),
                              r=[f"ps_c{jj}"], w=["vstg0"])
                        em.dma("gpsimd",
                               s_vt[P * 4 * g:P * 4 * (g + 1), P * c:P * (c + 1)].rearrange("(q p) n -> p q n", q=4),
                               vstg[jj][:].rearrange("p (q n) -> p q n", q=4),
                               r=["vstg0"], w=["s_vt"])

                    for t in range(8):
                        em.op("tensor", lambda EE, c=c, t=t, j=j: EE.matmul(
                            ps_g[:], cqk_b[:, 8 * c:8 * (c + 1)], xab_b[j][:, 512 * t:512 * (t + 1)],
                            start=True, stop=False),
                            r=["cqk_b", f"xab{j}"], w=["ps_g"])
                        em.op("tensor", lambda EE, c=c, t=t, j=j: EE.matmul(
                            ps_g[:], cv_b[:, 8 * c:8 * (c + 1)], xmb_b[j][:, 4 + 512 * t:4 + 512 * (t + 1)],
                            start=False, stop=True),
                            r=["cv_b", f"xmb{j}"], w=["ps_g"])
                        if c == 0:
                            em.op("vector", lambda EE, t=t: EE.tensor_copy(
                                gacc[1][:, 512 * t:512 * (t + 1)], ps_g[:]),
                                r=["ps_g"], w=[f"ga1_{t}"])
                        else:
                            sj, dj = (1, 0) if c % 2 == 1 else (0, 1)
                            em.op("vector", lambda EE, t=t, sj=sj, dj=dj: EE.tensor_tensor(
                                gacc[dj][:, 512 * t:512 * (t + 1)],
                                gacc[sj][:, 512 * t:512 * (t + 1)], ps_g[:], ALU.add),
                                r=["ps_g", f"ga{sj}_{t}"], w=[f"ga{dj}_{t}"])
                em.barrier()

        with contextlib.ExitStack() as ph3:
            g2 = SB(ph3, "g2", [8, S])
            gf = SB(ph3, "gf", [4, S])
            gwk = SB(ph3, "gwk", [4, S])
            gwk2 = SB(ph3, "gwk2", [4, S])
            lfc = SB(ph3, "lfc", [4, S])
            av = SB(ph3, "av", [4, S])
            mv = SB(ph3, "mv", [4, S])
            zz4 = SB(ph3, "zz4", [4, S])
            gfin = gacc[0]
            em.op("vector", lambda EE: EE.memset(zz4[:], 0.0), w=["zz4"])
            em.op("vector", lambda EE: EE.tensor_scalar(
                g2[:], gfin[:], bigfg_s[:], None, ALU.add),
                r=["bigfg_s"] + [f"ga0_{t}" for t in range(8)], w=["g2"])
            em.dma("sync", gf[:], g2[4:8, :], r=["g2"], w=["gf"])
            em.op("scalar", lambda EE: EE.activation(gwk[:], gf[:], AF.Abs),
                  r=["gf"], w=["gwk"])
            em.op("scalar", lambda EE: EE.activation(gwk2[:], gwk[:], AF.Exp, scale=-1.0),
                  r=["gwk"], w=["gwk2"])
            em.op("scalar", lambda EE: EE.activation(gwk[:], gwk2[:], AF.Ln, bias=epsb_s[0:4, 2:3]),
                  r=["gwk2", "epsb_s"], w=["gwk"])
            em.op("vector", lambda EE: EE.tensor_scalar(gwk2[:], gf[:], 0.0, None, ALU.min),
                  r=["gf"], w=["gwk2"])
            em.op("vector", lambda EE: EE.tensor_tensor(gf[:], gwk2[:], gwk[:], ALU.subtract),
                  r=["gwk", "gwk2"], w=["gf"])
            em.op("vector", lambda EE: EE.tensor_tensor_scan(
                lfc[:], gf[:], zz4[:], 0.0, ALU.add, ALU.add),
                r=["gf", "zz4"], w=["lfc"])
            em.op("vector", lambda EE: EE.tensor_tensor(av[:], g2[0:4, :], lfc[:], ALU.subtract),
                  r=["g2", "lfc"], w=["av"])
            em.op("vector", lambda EE: EE.tensor_tensor_scan(
                mv[:], av[:], av[:], -1e30, ALU.max, ALU.max),
                r=["av"], w=["mv"])
            em.op("vector", lambda EE: EE.tensor_tensor(gwk[:], lfc[:], mv[:], ALU.add),
                  r=["lfc", "mv"], w=["gwk"])
            em.op("scalar", lambda EE: EE.activation(gwk2[:], gwk[:], AF.Exp, scale=-1.0),
                  r=["gwk"], w=["gwk2"])
            em.dma("gpsimd", s_gat[0], av[:], r=["av"], w=["s_gat"])
            em.dma("gpsimd", s_gat[1], mv[:], r=["mv"], w=["s_gat"])
            em.dma("gpsimd", s_gat[2], gwk2[:], r=["gwk2"], w=["s_gat"])
            em.barrier()

        with contextlib.ExitStack() as ph4:
            masks_s = SB(ph4, "masks_s", [P, 4 * 512])
            em.dma("sync", masks_s[:].rearrange("p (r n) -> p r n", r=4), masks.rearrange("r p n -> p r n"), w=["masks_s"])
            ktb = [SB(ph4, f"ktb{d}", [P, S], BF16) for d in range(4)]
            vtb = SB(ph4, "vtb", [P, 32 * 512], BF16)
            qtb = [SB(ph4, f"qtb{d}", [P, 512], BF16) for d in range(4)]
            arow = SB(ph4, "arow", [32, P])
            aT = SB(ph4, "aT", [P, 32])
            mfb = SB(ph4, "mfb", [1, S])
            nmfb = SB(ph4, "nmfb", [1, S])
            mbs = SB(ph4, "mbs", [P, 512])
            mbm = SB(ph4, "mbm", [P, 512])
            ee_b = [SB(ph4, f"ee{i}", [P, 512]) for i in range(2)]
            ce_b = [SB(ph4, f"ce{i}", [P, 512], BF16) for i in range(2)]
            hf_b = [SB(ph4, f"hf{d}", [P, 512]) for d in range(4)]
            hq_b = [SB(ph4, f"hq{d}", [P, 512]) for d in range(4)]
            e1 = SB(ph4, "e1", [1, 512 * 8])
            anw_s = SB(ph4, "anw_s", [P, 512])
            xas = [SB(ph4, f"xas{i}", [P, 512], BF16) for i in range(2)]
            szs = [SB(ph4, f"szs{i}", [P, 512], BF16) for i in range(2)]
            hst_stg = [SB(ph4, f"hstg{i}", [P, 512], BF16) for i in range(2)]
            ps_ct = [PS(ph4, f"ps_ct{i}", [P, 512]) for i in range(2)]
            ps_h = [PS(ph4, f"ps_h{d}", [P, 512]) for d in range(4)]
            ps_r = PS(ph4, "ps_r", [1, 512])
            ps_x = PS(ph4, "ps_x", [P, 512])

            for h in range(NH):
                for d in range(4):
                    em.dma("sync", ktb[d][:], s_kT[DH * h + P * d:DH * h + P * (d + 1), :],
                           r=["s_kT"], w=[f"ktb{d}"])
                for jt in range(32):
                    em.dma("sync", vtb[:, 512 * jt:512 * (jt + 1)],
                           s_vt[P * jt:P * (jt + 1), DH * h:DH * (h + 1)],
                           r=["s_vt"], w=["vtb"])
                em.dma("sync", arow[:], s_gat[0, h].rearrange("(j n) -> j n", j=32),
                       r=["s_gat"], w=["arow"])
                em.op("tensor", lambda EE: EE.transpose(ps_x[:, 0:32], arow[:], ident_s[0:32, 0:32]),
                      r=["arow", "ident_s"], w=["ps_x"])
                em.op("vector", lambda EE: EE.tensor_copy(aT[:], ps_x[:, 0:32]),
                      r=["ps_x"], w=["aT"])
                em.dma("sync", mfb[:], s_gat[1, h].rearrange("(o n) -> o n", o=1), r=["s_gat"], w=["mfb"])
                em.dma("sync", nmfb[:], s_gat[2, h].rearrange("(o n) -> o n", o=1), r=["s_gat"], w=["nmfb"])

                for qi in range(8):
                    for d in range(4):
                        em.dma("sync", qtb[d][:],
                               s_qT[DH * h + P * d:DH * h + P * (d + 1), 512 * qi:512 * (qi + 1)],
                               r=["s_qT"], w=[f"qtb{d}"])
                    em.op("tensor", lambda EE, qi=qi: EE.matmul(
                        ps_x[:], ones_r[:], mfb[0:1, 512 * qi:512 * (qi + 1)],
                        start=True, stop=True), r=["ones_r", "mfb"], w=["ps_x"])
                    em.op("vector", lambda EE: EE.tensor_copy(mbs[:], ps_x[:]),
                          r=["ps_x"], w=["mbs"])
                    nkj = 4 * qi + 4

                    def emit_c(kj):
                        i = kj % 2
                        for d in range(4):
                            em.op("tensor", lambda EE, kj=kj, d=d, i=i: EE.matmul(
                                ps_ct[i][:], ktb[d][:, P * kj:P * (kj + 1)], qtb[d][:],
                                start=(d == 0), stop=(d == 3)),
                                r=[f"ktb{d}", f"qtb{d}"], w=[f"ps_ct{i}"])
                    emit_c(0)
                    for kj in range(nkj):
                        i = kj % 2
                        rdiag = kj - 4 * qi
                        if kj + 1 < nkj:
                            emit_c(kj + 1)
                        if rdiag >= 0:
                            em.op("vector", lambda EE, rdiag=rdiag: EE.tensor_tensor(
                                mbm[:], mbs[:], masks_s[:, 512 * rdiag:512 * (rdiag + 1)], ALU.add),
                                r=["mbs", "masks_s"], w=["mbm"])
                            em.op("scalar", lambda EE, kj=kj, i=i: EE.activation(
                                ee_b[i][:], mbm[:], AF.Exp, bias=aT[:, kj:kj + 1], scale=-1.0),
                                r=["mbm", "aT"], w=[f"ee{i}"])
                        else:
                            em.op("scalar", lambda EE, kj=kj, i=i: EE.activation(
                                ee_b[i][:], mbs[:], AF.Exp, bias=aT[:, kj:kj + 1], scale=-1.0),
                                r=["mbs", "aT"], w=[f"ee{i}"])
                        em.op("vector", lambda EE, i=i: EE.tensor_tensor(
                            ce_b[i][:], ps_ct[i][:], ee_b[i][:], ALU.mult),
                            r=[f"ps_ct{i}", f"ee{i}"], w=[f"ce{i}"])
                        for d in range(4):
                            em.op("tensor", lambda EE, kj=kj, d=d, i=i, nkj=nkj: EE.matmul(
                                ps_h[d][:], vtb[:, 512 * kj + P * d:512 * kj + P * (d + 1)], ce_b[i][:],
                                start=(kj == 0), stop=(kj == nkj - 1)),
                                r=["vtb", f"ce{i}"], w=[f"ps_h{d}"])
                        em.op("tensor", lambda EE, kj=kj, i=i, nkj=nkj: EE.matmul(
                            ps_r[:], ones_cb[:], ce_b[i][:],
                            start=(kj == 0), stop=(kj == nkj - 1)),
                            r=["ones_cb", f"ce{i}"], w=["ps_r"])

                    em.op("scalar", lambda EE: EE.activation(
                        e1[0:1, 0:512], ps_r[:], AF.Abs),
                        r=["ps_r"], w=["e1A"])
                    em.op("vector", lambda EE, qi=qi: EE.tensor_tensor(
                        e1[0:1, 512:1024], e1[0:1, 0:512],
                        nmfb[0:1, 512 * qi:512 * (qi + 1)], ALU.max),
                        r=["e1A", "nmfb"], w=["e1B"])
                    em.op("vector", lambda EE: EE.tensor_scalar(
                        e1[0:1, 1024:1536], e1[0:1, 512:1024], 1e-6, None, ALU.add),
                        r=["e1B"], w=["e1C"])
                    em.op("vector", lambda EE: EE.reciprocal(e1[0:1, 1536:2048], e1[0:1, 1024:1536]),
                          r=["e1C"], w=["e1D"])
                    em.op("tensor", lambda EE: EE.matmul(
                        ps_x[:], ones_r[:], e1[0:1, 1536:2048], start=True, stop=True),
                        r=["ones_r", "e1D"], w=["ps_x"])
                    em.op("vector", lambda EE: EE.tensor_copy(mbs[:], ps_x[:]),
                          r=["ps_x"], w=["mbs"])
                    for d in range(4):
                        em.op("vector", lambda EE, d=d: EE.tensor_tensor(
                            hf_b[d][:], ps_h[d][:], mbs[:], ALU.mult),
                            r=[f"ps_h{d}", "mbs"], w=[f"hf{d}"])
                        em.op("scalar", lambda EE, d=d: EE.activation(
                            hq_b[d][:], hf_b[d][:], AF.Square),
                            r=[f"hf{d}"], w=[f"hq{d}"])
                    for d in range(4):
                        em.op("tensor", lambda EE, d=d: EE.matmul(
                            ps_r[:], ones_cf[:], hf_b[d][:], start=(d == 0), stop=(d == 3)),
                            r=["ones_cf", f"hf{d}"], w=["ps_r"])
                    em.op("vector", lambda EE: EE.tensor_scalar(
                        e1[0:1, 2048:2560], ps_r[:], 1.0 / DH, None, ALU.mult),
                        r=["ps_r"], w=["e1MU"])
                    for d in range(4):
                        em.op("tensor", lambda EE, d=d: EE.matmul(
                            ps_r[:], ones_cf[:], hq_b[d][:], start=(d == 0), stop=(d == 3)),
                            r=["ones_cf", f"hq{d}"], w=["ps_r"])
                    em.op("vector", lambda EE: EE.tensor_scalar(
                        e1[0:1, 2560:3072], ps_r[:], 1.0 / DH, None, ALU.mult),
                        r=["ps_r"], w=["e1MS"])
                    em.op("vector", lambda EE: EE.tensor_tensor(
                        e1[0:1, 3072:3584], e1[0:1, 2048:2560], e1[0:1, 2048:2560], ALU.mult),
                        r=["e1MU"], w=["e1M2"])
                    em.op("vector", lambda EE: EE.tensor_tensor(
                        e1[0:1, 3584:4096], e1[0:1, 2560:3072], e1[0:1, 3072:3584], ALU.subtract),
                        r=["e1MS", "e1M2"], w=["e1VA"])
                    em.op("scalar", lambda EE: EE.activation(
                        e1[0:1, 0:512], e1[0:1, 3584:4096], AF.Sqrt, bias=epsb_s[0:1, 1:2]),
                        r=["e1VA", "epsb_s"], w=["e1A"])
                    em.op("vector", lambda EE: EE.reciprocal(e1[0:1, 512:1024], e1[0:1, 0:512]),
                          r=["e1A"], w=["e1B"])
                    em.op("vector", lambda EE: EE.tensor_tensor(
                        e1[0:1, 1024:1536], e1[0:1, 2048:2560], e1[0:1, 512:1024], ALU.mult),
                        r=["e1MU", "e1B"], w=["e1C"])
                    for d in range(4):
                        ch = 4 * h + d
                        em.op("tensor", lambda EE, ch=ch: EE.matmul(
                            ps_x[:], nwrow_s[0:1, P * ch:P * (ch + 1)], e1[0:1, 512:1024],
                            start=True, stop=True), r=["nwrow_s", "e1B"], w=["ps_x"])
                        em.op("vector", lambda EE: EE.tensor_copy(anw_s[:], ps_x[:]),
                              r=["ps_x"], w=["anw_s"])
                        em.op("tensor", lambda EE, ch=ch: EE.matmul(
                            ps_x[:], nwrow_s[0:1, P * ch:P * (ch + 1)], e1[0:1, 1024:1536],
                            start=True, stop=True), r=["nwrow_s", "e1C", "anw_s"], w=["ps_x"])
                        i2 = d % 2
                        em.dma("sync", xas[i2][:], s_xact[P * ch:P * (ch + 1), 512 * qi:512 * (qi + 1)],
                               r=["s_xact"], w=[f"xas{i2}"])
                        em.dma("sync", szs[i2][:], s_swz[P * ch:P * (ch + 1), 512 * qi:512 * (qi + 1)],
                               r=["s_swz"], w=[f"szs{i2}"])
                        em.op("vector", lambda EE, ch=ch, i2=i2: EE.scalar_tensor_tensor(
                            ee_b[0][:], xas[i2][:], skip2d_s[:, ch:ch + 1], ps_x[:],
                            ALU.mult, ALU.subtract),
                            r=[f"xas{i2}", "skip2d_s", "ps_x"], w=["ee0"])
                        em.op("vector", lambda EE, d=d: EE.tensor_tensor(
                            ee_b[1][:], hf_b[d][:], anw_s[:], ALU.mult),
                            r=[f"hf{d}", "anw_s"], w=["ee1"])
                        em.op("vector", lambda EE: EE.tensor_tensor(
                            mbm[:], ee_b[1][:], ee_b[0][:], ALU.add),
                            r=["ee0", "ee1"], w=["mbm"])
                        em.op("vector", lambda EE, i2=i2: EE.tensor_tensor(
                            hst_stg[i2][:], mbm[:], szs[i2][:], ALU.mult),
                            r=["mbm", f"szs{i2}"], w=[f"hstg{i2}"])
                        em.dma("gpsimd", s_hst[P * ch:P * (ch + 1), 512 * qi:512 * (qi + 1)],
                               hst_stg[i2][:], r=[f"hstg{i2}"], w=["s_hst"])
            em.barrier()

        with contextlib.ExitStack() as ph5:
            wdf = [SB(ph5, f"wdf{i}", [P, E]) for i in range(2)]
            NB5 = 4
            wdb = [SB(ph5, f"wdb{c}", [P, E], BF16) for c in range(16)]
            hs = [SB(ph5, f"hs{c}", [P, IN], BF16) for c in range(16)]
            ao = [SB(ph5, f"ao{i}", [P, E]) for i in range(4)]
            aq = [SB(ph5, f"aq{i % 2}", [P, E]) for i in range(2)] * 2
            xr = [SB(ph5, f"xr{i % 2}", [P, E]) for i in range(2)] * 2
            y1s = [SB(ph5, f"y1s{i % 2}", [P, E]) for i in range(2)] * 2
            st5 = SB(ph5, "st5", [P, 32])
            ps_d = [PS(ph5, f"ps_d{i}", [P, 512]) for i in range(8)]

            for c in range(16):
                i = c % 2
                em.dma("sync", wdf[i][:], wdn[P * c:P * (c + 1), :], w=[f"wdf{i}"])
                em.op("scalar", lambda EE, c=c, i=i: EE.activation(wdb[c][:], wdf[i][:], AF.Copy),
                      r=[f"wdf{i}"], w=[f"wdb{c}"])
            wst0 = SB(ph5, "wst0", [P, HID], I32)
            wst = [wst0, wst0]
            wsb0 = SB(ph5, "wsb0", [P, HID], BF16)
            wsb = [wsb0, wsb0]
            for idx, (src, dst) in enumerate([(wgT, s_wg16), (wuT, s_wu16)]):
                for c in range(8):
                    i = c % 2
                    em.dma("sync", wst[i][:], src[P * c:P * (c + 1), :], w=["wst0"])
                    em.op("scalar", lambda EE, i=i: EE.activation(wsb[i][:], wst[i][:], AF.Copy, scale=DEQ),
                          r=["wst0"], w=["wsb0"])
                    em.dma("gpsimd", dst[P * c:P * (c + 1), :], wsb[i][:],
                           r=["wsb0"], w=["s_w16"])
            for c in range(24):
                i = c % 2
                em.dma("sync", wst[i][:, 0:E], wdT[P * c:P * (c + 1), :], w=["wst0"])
                em.op("scalar", lambda EE, i=i: EE.activation(wsb[i][:, 0:E], wst[i][:, 0:E], AF.Copy, scale=DEQ),
                      r=["wst0"], w=["wsb0"])
                em.dma("gpsimd", s_wd16[P * c:P * (c + 1), :], wsb[i][:, 0:E],
                       r=["wsb0"], w=["s_w16"])

            for half in range(2):
                for c in range(16):
                    em.dma("sync", hs[c][:], s_hst[P * c:P * (c + 1), IN * half:IN * (half + 1)],
                           r=["s_hst"], w=[f"hs{c}"])
                for tt in range(16):
                    i = tt % 4
                    for c in range(16):
                        em.op("tensor", lambda EE, c=c, tt=tt, i=i: EE.matmul(
                            ps_d[2 * i][:], hs[c][:, P * tt:P * (tt + 1)], wdb[c][:, 0:512],
                            start=(c == 0), stop=(c == 15)),
                            r=[f"hs{c}", f"wdb{c}"], w=[f"ps_d{2 * i}"])
                        em.op("tensor", lambda EE, c=c, tt=tt, i=i: EE.matmul(
                            ps_d[2 * i + 1][:], hs[c][:, P * tt:P * (tt + 1)], wdb[c][:, 512:1024],
                            start=(c == 0), stop=(c == 15)),
                            r=[f"hs{c}", f"wdb{c}"], w=[f"ps_d{2 * i + 1}"])
                    em.op("vector", lambda EE, i=i: EE.tensor_copy(ao[i][:, 0:512], ps_d[2 * i][:]),
                          r=[f"ps_d{2 * i}"], w=[f"ao{i}"])
                    em.op("vector", lambda EE, i=i: EE.tensor_copy(ao[i][:, 512:1024], ps_d[2 * i + 1][:]),
                          r=[f"ps_d{2 * i + 1}"], w=[f"ao{i}"])
                    row = P * (16 * half + tt)
                    em.dma("sync", xr[i][:], x[row:row + P, :], w=[f"xr{i % 2}"])
                    em.op("vector", lambda EE, i=i: EE.reduce_sum(st5[:, 8 * i + 0:8 * i + 1], ao[i][:], AX.X),
                          r=[f"ao{i}"], w=[f"sb{i}_0"])
                    em.op("scalar", lambda EE, i=i: EE.activation(aq[i][:], ao[i][:], AF.Square),
                          r=[f"ao{i}"], w=[f"aq{i % 2}"])
                    em.op("vector", lambda EE, i=i: EE.reduce_sum(st5[:, 8 * i + 1:8 * i + 2], aq[i][:], AX.X),
                          r=[f"aq{i % 2}"], w=[f"sb{i}_1"])
                    em.op("vector", lambda EE, i=i: EE.tensor_scalar(st5[:, 8 * i + 2:8 * i + 3], st5[:, 8 * i + 0:8 * i + 1], 1.0 / E, None, ALU.mult),
                          r=[f"sb{i}_0"], w=[f"sb{i}_2"])
                    em.op("vector", lambda EE, i=i: EE.tensor_scalar(st5[:, 8 * i + 3:8 * i + 4], st5[:, 8 * i + 1:8 * i + 2], 1.0 / E, None, ALU.mult),
                          r=[f"sb{i}_1"], w=[f"sb{i}_3"])
                    em.op("vector", lambda EE, i=i: EE.tensor_tensor(st5[:, 8 * i + 4:8 * i + 5], st5[:, 8 * i + 2:8 * i + 3], st5[:, 8 * i + 2:8 * i + 3], ALU.mult),
                          r=[f"sb{i}_2"], w=[f"sb{i}_4"])
                    em.op("vector", lambda EE, i=i: EE.tensor_tensor(st5[:, 8 * i + 5:8 * i + 6], st5[:, 8 * i + 3:8 * i + 4], st5[:, 8 * i + 4:8 * i + 5], ALU.subtract),
                          r=[f"sb{i}_3", f"sb{i}_4"], w=[f"sb{i}_5"])
                    em.op("scalar", lambda EE, i=i: EE.activation(st5[:, 8 * i + 6:8 * i + 7], st5[:, 8 * i + 5:8 * i + 6], AF.Sqrt, bias=epsb_s[:, 0:1]),
                          r=[f"sb{i}_5", "epsb_s"], w=[f"sb{i}_6"])
                    em.op("vector", lambda EE, i=i: EE.reciprocal(st5[:, 8 * i + 7:8 * i + 8], st5[:, 8 * i + 6:8 * i + 7]),
                          r=[f"sb{i}_6"], w=[f"sb{i}_7"])
                    em.op("vector", lambda EE, i=i: EE.tensor_scalar(
                        ao[i][:], ao[i][:], st5[:, 8 * i + 2:8 * i + 3], st5[:, 8 * i + 7:8 * i + 8], ALU.subtract, ALU.mult),
                        r=[f"sb{i}_2", f"sb{i}_7"], w=[f"ao{i}"])
                    em.op("vector", lambda EE, i=i: EE.tensor_tensor(
                        y1s[i][:], ao[i][:], xr[i][:], ALU.add),
                        r=[f"ao{i}", f"xr{i % 2}"], w=[f"y1s{i % 2}"])
                    em.dma("gpsimd", s_y1[row:row + P, :], y1s[i][:],
                           r=[f"y1s{i % 2}"], w=["s_y1"])
            em.barrier()

        def sw_fn(EE, sem, amt):
            for arm in nc.switch_core_id(engines=[EE], n=128):
                try:
                    hg = arm.logical % 2
                except Exception:
                    hg = int(arm) % 2
                EE.dma_start(s_y1own, s_y1[2048 * hg:2048 * (hg + 1), :]).then_inc(sem, amt)
        em.dma_op("sync", sw_fn, r=["s_y1"], w=["s_y1own"], self_inc=True)
        em.barrier()

        with contextlib.ExitStack() as ph6w:
            y1l = [SB(ph6w, f"y1l{i}", [P, E]) for i in range(2)]
            y1q = [SB(ph6w, f"y1q{i}", [P, E]) for i in range(2)]
            lno = [SB(ph6w, f"lno{i}", [P, E], BF16) for i in range(2)]
            st6 = SB(ph6w, "st6", [P, 8])
            for t in range(16):
                i = t % 2
                em.dma("sync", y1l[i][:], s_y1own[P * t:P * (t + 1), :], r=["s_y1own"], w=[f"y1l{i}"])
                em.op("vector", lambda EE, i=i: EE.reduce_sum(st6[:, 0:1], y1l[i][:], AX.X),
                      r=[f"y1l{i}"], w=["sc0"])
                em.op("scalar", lambda EE, i=i: EE.activation(y1q[i][:], y1l[i][:], AF.Square),
                      r=[f"y1l{i}"], w=[f"y1q{i}"])
                em.op("vector", lambda EE, i=i: EE.reduce_sum(st6[:, 1:2], y1q[i][:], AX.X),
                      r=[f"y1q{i}"], w=["sc1"])
                em.op("vector", lambda EE: EE.tensor_scalar(st6[:, 2:3], st6[:, 0:1], 1.0 / E, None, ALU.mult),
                      r=["sc0"], w=["sc2"])
                em.op("vector", lambda EE: EE.tensor_scalar(st6[:, 3:4], st6[:, 1:2], 1.0 / E, None, ALU.mult),
                      r=["sc1"], w=["sc3"])
                em.op("vector", lambda EE: EE.tensor_tensor(st6[:, 4:5], st6[:, 2:3], st6[:, 2:3], ALU.mult),
                      r=["sc2"], w=["sc4"])
                em.op("vector", lambda EE: EE.tensor_tensor(st6[:, 5:6], st6[:, 3:4], st6[:, 4:5], ALU.subtract),
                      r=["sc3", "sc4"], w=["sc5"])
                em.op("scalar", lambda EE: EE.activation(st6[:, 6:7], st6[:, 5:6], AF.Sqrt, bias=epsb_s[:, 0:1]),
                      r=["sc5", "epsb_s"], w=["sc6"])
                em.op("vector", lambda EE: EE.reciprocal(st6[:, 7:8], st6[:, 6:7]),
                      r=["sc6"], w=["sc7"])
                em.op("vector", lambda EE, i=i: EE.tensor_scalar(
                    lno[i][:], y1l[i][:], st6[:, 2:3], st6[:, 7:8], ALU.subtract, ALU.mult),
                    r=[f"y1l{i}", "sc2", "sc7"], w=[f"lno{i}"])
                em.dma("gpsimd", s_ln2[P * t:P * (t + 1), :], lno[i][:],
                       r=[f"lno{i}"], w=["s_ln2"])
            em.barrier()

        with contextlib.ExitStack() as ph6:
            ln2T = [SB(ph6, f"ln2T{c}", [P, IN], BF16, is_transpose_dest=True) for c in range(8)]
            for c in range(8):
                em.dma_op("sync", lambda EE, c=c: EE.dma_start_transpose(
                    ln2T[c][:], s_ln2[:, P * c:P * (c + 1)]), r=["s_ln2"], w=[f"ln2T{c}"])
            wd16 = [SB(ph6, f"wd16_{c}", [P, E], BF16) for c in range(24)]
            for c in range(24):
                em.dma("sync", wd16[c][:], s_wd16[P * c:P * (c + 1), :], r=["s_w16"], w=[f"wd16_{c}"])
            wgs = [SB(ph6, f"wgs{i}", [P, 8 * P], BF16) for i in range(2)]
            wus = [SB(ph6, f"wus{i}", [P, 8 * P], BF16) for i in range(2)]
            hidT = [SB(ph6, f"hidT{c}", [P, 512], BF16) for c in range(24)]
            sg_b = SB(ph6, "sg_b", [P, 512], BF16)
            fo = [SB(ph6, f"fo{i}", [P, E]) for i in range(2)]
            fq = [SB(ph6, f"fq{i}", [P, E]) for i in range(2)]
            yo = [SB(ph6, f"yo{i}", [P, E]) for i in range(2)]
            y1r = [SB(ph6, f"y1r{i}", [P, E]) for i in range(2)]
            st7 = SB(ph6, "st7", [P, 8])
            ps_gt = PS(ph6, "ps_gt", [P, 512])
            ps_ut = PS(ph6, "ps_ut", [P, 512])
            ps_f = [PS(ph6, f"ps_f{i}", [P, 512]) for i in range(4)]

            for ti in range(4):
                for hh in range(24):
                    i = hh % 2
                    em.dma("sync", wgs[i][:].rearrange("p (k q) -> p k q", k=8),
                           s_wg16.rearrange("(k p) q -> p k q", p=P)[:, :, P * hh:P * (hh + 1)],
                           r=["s_w16"], w=[f"wgs{i}"])
                    em.dma("sync", wus[i][:].rearrange("p (k q) -> p k q", k=8),
                           s_wu16.rearrange("(k p) q -> p k q", p=P)[:, :, P * hh:P * (hh + 1)],
                           r=["s_w16"], w=[f"wus{i}"])
                    for k in range(8):
                        em.op("tensor", lambda EE, hh=hh, k=k, ti=ti, i=i: EE.matmul(
                            ps_gt[:], wgs[i][:, P * k:P * (k + 1)], ln2T[k][:, 512 * ti:512 * (ti + 1)],
                            start=(k == 0), stop=(k == 7)),
                            r=[f"wgs{i}", f"ln2T{k}"], w=["ps_gt"])
                        em.op("tensor", lambda EE, hh=hh, k=k, ti=ti, i=i: EE.matmul(
                            ps_ut[:], wus[i][:, P * k:P * (k + 1)], ln2T[k][:, 512 * ti:512 * (ti + 1)],
                            start=(k == 0), stop=(k == 7)),
                            r=[f"wus{i}", f"ln2T{k}"], w=["ps_ut"])
                    em.op("scalar", lambda EE: EE.activation(sg_b[:], ps_gt[:], AF.Silu),
                          r=["ps_gt"], w=["sg_b"])
                    em.op("vector", lambda EE, hh=hh: EE.tensor_tensor(
                        hidT[hh][:], ps_ut[:], sg_b[:], ALU.mult),
                        r=["ps_ut", "sg_b"], w=[f"hidT{hh}"])
                for tt in range(4):
                    i = tt % 2
                    for hh in range(24):
                        em.op("tensor", lambda EE, hh=hh, tt=tt, i=i: EE.matmul(
                            ps_f[2 * i][:], hidT[hh][:, P * tt:P * (tt + 1)], wd16[hh][:, 0:512],
                            start=(hh == 0), stop=(hh == 23)),
                            r=[f"hidT{hh}", f"wd16_{hh}"], w=[f"ps_f{2 * i}"])
                        em.op("tensor", lambda EE, hh=hh, tt=tt, i=i: EE.matmul(
                            ps_f[2 * i + 1][:], hidT[hh][:, P * tt:P * (tt + 1)], wd16[hh][:, 512:1024],
                            start=(hh == 0), stop=(hh == 23)),
                            r=[f"hidT{hh}", f"wd16_{hh}"], w=[f"ps_f{2 * i + 1}"])
                    em.op("vector", lambda EE, i=i: EE.tensor_copy(fo[i][:, 0:512], ps_f[2 * i][:]),
                          r=[f"ps_f{2 * i}"], w=[f"fo{i}"])
                    em.op("vector", lambda EE, i=i: EE.tensor_copy(fo[i][:, 512:1024], ps_f[2 * i + 1][:]),
                          r=[f"ps_f{2 * i + 1}"], w=[f"fo{i}"])
                    row = P * (4 * ti + tt)
                    em.dma("sync", y1r[i][:], s_y1own[row:row + P, :], r=["s_y1own"], w=[f"y1r{i}"])
                    em.op("vector", lambda EE, i=i: EE.reduce_sum(st7[:, 0:1], fo[i][:], AX.X),
                          r=[f"fo{i}"], w=["sd0"])
                    em.op("scalar", lambda EE, i=i: EE.activation(fq[i][:], fo[i][:], AF.Square),
                          r=[f"fo{i}"], w=[f"fq{i}"])
                    em.op("vector", lambda EE, i=i: EE.reduce_sum(st7[:, 1:2], fq[i][:], AX.X),
                          r=[f"fq{i}"], w=["sd1"])
                    em.op("vector", lambda EE: EE.tensor_scalar(st7[:, 2:3], st7[:, 0:1], 1.0 / E, None, ALU.mult),
                          r=["sd0"], w=["sd2"])
                    em.op("vector", lambda EE: EE.tensor_scalar(st7[:, 3:4], st7[:, 1:2], 1.0 / E, None, ALU.mult),
                          r=["sd1"], w=["sd3"])
                    em.op("vector", lambda EE: EE.tensor_tensor(st7[:, 4:5], st7[:, 2:3], st7[:, 2:3], ALU.mult),
                          r=["sd2"], w=["sd4"])
                    em.op("vector", lambda EE: EE.tensor_tensor(st7[:, 5:6], st7[:, 3:4], st7[:, 4:5], ALU.subtract),
                          r=["sd3", "sd4"], w=["sd5"])
                    em.op("scalar", lambda EE: EE.activation(st7[:, 6:7], st7[:, 5:6], AF.Sqrt, bias=epsb_s[:, 0:1]),
                          r=["sd5", "epsb_s"], w=["sd6"])
                    em.op("vector", lambda EE: EE.reciprocal(st7[:, 7:8], st7[:, 6:7]),
                          r=["sd6"], w=["sd7"])
                    em.op("vector", lambda EE, i=i: EE.tensor_scalar(
                        fo[i][:], fo[i][:], st7[:, 2:3], st7[:, 7:8], ALU.subtract, ALU.mult),
                        r=["sd2", "sd7"], w=[f"fo{i}"])
                    em.op("vector", lambda EE, i=i: EE.tensor_tensor(
                        yo[i][:], fo[i][:], y1r[i][:], ALU.add),
                        r=[f"fo{i}", f"y1r{i}"], w=[f"yo{i}"])
                    em.dma("gpsimd", y[row:row + P, :], yo[i][:], r=[f"yo{i}"], w=["y_out"])

            fence_waits = [(f"q_gpsimd{i}", em.cnt[f"q_gpsimd{i}"]) for i in range(NQ)]
            fence_waits += [(f"q_sync{i}", em.cnt[f"q_sync{i}"]) for i in range(NQ)]
            em.op("tensor", lambda EE: EE.drain(), extra=fence_waits)

        with nc.Block() as block:
            em.replay(block)
    return nc


def stage_inputs(inputs):
    g = lambda k: np.asarray(inputs[k])
    x_full = g("L0_attn_x").astype(np.float32)
    wup = np.ascontiguousarray(g("L0_attn_W_up"))
    conv_w = g("L0_attn_conv_w")
    conv_b = g("L0_attn_conv_b")
    Wq = g("L0_attn_Wq"); Wk = g("L0_attn_Wk"); Wv = g("L0_attn_Wv")
    W_ig = g("L0_attn_W_ig"); W_fg = g("L0_attn_W_fg")
    b_ig = g("L0_attn_b_ig"); b_fg = g("L0_attn_b_fg")
    norm_w = g("L0_attn_norm_w"); skip = g("L0_attn_skip")
    wdn = np.ascontiguousarray(g("L0_attn_W_down"))
    wg = g("L0_ffn_w_gate"); wu = g("L0_ffn_w_up"); wd = g("L0_ffn_w_down")

    convdiag = np.zeros((16, KCV, P, P), np.float32)
    for c in range(16):
        for i in range(KCV):
            np.fill_diagonal(convdiag[c, i], conv_w[i, P * c:P * (c + 1)])
    convb2d = np.ascontiguousarray(conv_b.reshape(16, P).T).astype(np.float32)

    def blockdiag(W, transposed):
        out = np.zeros((16, P, P), np.float32)
        for c in range(16):
            for gidx in range(32):
                blk = W[32 * c + gidx]
                if transposed:
                    out[c, 4 * gidx:4 * gidx + 4, 4 * gidx:4 * gidx + 4] = blk
                else:
                    out[c, 4 * gidx:4 * gidx + 4, 4 * gidx:4 * gidx + 4] = blk.T
        return out

    bdq_ = blockdiag(Wq, False); bdk_ = blockdiag(Wk, False); bdv_ = blockdiag(Wv, False)
    bdqt_ = blockdiag(Wq, True); bdkt_ = blockdiag(Wk, True); bdvt_ = blockdiag(Wv, True)

    wigfg = np.zeros((3 * IN, 8), np.float32)
    wigfg[:, 0:4] = W_ig
    wigfg[:, 4:8] = W_fg
    bigfg = np.concatenate([b_ig, b_fg]).reshape(8, 1).astype(np.float32)
    nw2d = np.ascontiguousarray(norm_w.reshape(16, P).T).astype(np.float32)
    skip2d = np.ascontiguousarray(skip.reshape(16, P).T).astype(np.float32)
    nwrow = norm_w.reshape(1, IN).astype(np.float32)

    masks_h = np.zeros((4, P, 512), np.float32)
    pp = np.arange(P)[:, None]
    nn = np.arange(512)[None, :]
    for r in range(4):
        masks_h[r] = 30000.0 * (pp > nn - P * r)

    common = dict(
        wup=wup.astype(np.float32),
        convdiag=convdiag, convb2d=convb2d,
        bdq=bdq_, bdk=bdk_, bdv=bdv_, bdqt=bdqt_, bdkt=bdkt_, bdvt=bdvt_,
        wigfg=wigfg, bigfg=bigfg, nw2d=nw2d, nwrow=nwrow, skip2d=skip2d,
        wdn=wdn.astype(np.float32),
        wgT=np.ascontiguousarray(wg.T).astype(np.int32),
        wuT=np.ascontiguousarray(wu.T).astype(np.int32),
        wdT=np.ascontiguousarray(wd.T).astype(np.int32),
        masks=masks_h,
        ident=np.eye(P, dtype=np.float32),
        onesrow=np.ones((1, P), np.float32),
        onescol=np.ones((P, 1), np.float32),
        epsb=np.tile(np.array([[1e-5, 1e-6, 1.0, 0.0]], np.float32), (P, 1)),
    )
    in_maps = []
    for c in range(8):
        b = c // 2
        m = dict(common)
        m["x"] = np.ascontiguousarray(x_full[b])
        in_maps.append(m)
    return in_maps


def kernel(**inputs):
    in_maps = stage_inputs(inputs)
    nc = build_program()
    res = run_bass_kernel_spmd(nc, in_maps, core_ids=list(range(8)))
    out = np.empty((4, S, E), np.float32)
    for b in range(4):
        out[b, 0:2048] = res.results[2 * b]["y"]
        out[b, 2048:4096] = res.results[2 * b + 1]["y"]
    return out


# version 07
# speedup vs baseline: 1.0497x; 1.0497x over previous
import numpy as np
import contextlib
import concourse.bass as bass
import concourse.mybir as mybir
from concourse.bass_utils import run_bass_kernel_spmd

F32 = mybir.dt.float32
BF16 = mybir.dt.bfloat16
I32 = mybir.dt.int32
AF = mybir.ActivationFunctionType
ALU = mybir.AluOpType
AX = mybir.AxisListType

P = 128
S = 4096
E = 1024
IN = 2048
NH = 4
DH = 512
HID = 3072
KCV = 4
SC = float(DH) ** -0.5
DEQ = 1.0 / 127.0

ENGS = ["sync", "tensor", "vector", "scalar", "gpsimd"]
NQ = 4


class Emitter:

    def __init__(self, nc):
        self.nc = nc
        self.ops = {e: [] for e in ENGS}
        self.cnt = {}
        self.ndma = {e: 0 for e in ENGS}
        self.semnames = []
        for e in ENGS:
            self._reg(f"c_{e}")
            for i in range(NQ):
                self._reg(f"q_{e}{i}")
        self.sems = {}
        self.writers = {}
        self.readers = {}
        self.fence = None

    def _reg(self, name):
        self.semnames.append(name)
        self.cnt[name] = 0

    def make_sems(self, stack):
        for n in self.semnames:
            self.sems[n] = stack.enter_context(self.nc.semaphore(n))

    def _emit(self, eng, fn, r, w, extra, dma, self_inc):
        deps = []
        for nm in r:
            deps.extend(self.writers.get(nm, ()))
        for nm in w:
            rd = self.readers.get(nm, ())
            deps.extend(rd)
            if rd:
                self.writers[nm] = []
                self.readers[nm] = []
        deps.extend(extra)
        if self.fence is not None:
            deps.append(self.fence)
        if dma:
            k = self.ndma[eng]
            self.ndma[eng] += 1
            sem = f"q_{eng}{k % NQ}"
            deps.append((sem, self.cnt[sem]))
            amt = 16
        else:
            sem = f"c_{eng}"
            amt = 1
        self.ops[eng].append((deps, fn, sem, amt, self_inc, eng))
        self.cnt[sem] += amt
        tok = (sem, self.cnt[sem])
        for nm in r:
            self.readers.setdefault(nm, []).append(tok)
        for nm in w:
            self.writers.setdefault(nm, []).append(tok)
        return tok

    def op(self, eng, fn, r=(), w=(), extra=(), self_inc=False):
        return self._emit(eng, fn, r, w, list(extra), False, self_inc)

    def dma(self, eng, out_ap, in_ap, r=(), w=(), extra=(), **kw):
        return self._emit(eng, lambda EE: EE.dma_start(out_ap, in_ap, **kw),
                          r, w, list(extra), True, False)

    def dma_op(self, eng, fn, r=(), w=(), extra=(), self_inc=False):
        return self._emit(eng, fn, r, w, list(extra), True, self_inc)

    def barrier(self):
        marks = [(f"c_{e}", self.cnt[f"c_{e}"] + 1) for e in ENGS]
        qmarks = [(f"q_{e}{i}", self.cnt[f"q_{e}{i}"]) for e in ENGS for i in range(NQ)]
        def mk(e):
            return lambda EE: EE.drain()
        for e in ENGS:
            self.ops[e].append(([], mk(e), f"c_{e}", 1, False, e))
            self.cnt[f"c_{e}"] += 1
        waits = marks + [q for q in qmarks if q[1] > 0]
        toks = []
        for e in ENGS:
            self.ops[e].append((list(waits), mk(e), f"c_{e}", 1, False, e))
            self.cnt[f"c_{e}"] += 1
            toks.append((f"c_{e}", self.cnt[f"c_{e}"]))
        self.writers = {}
        self.readers = {}
        self.fence = toks[0]
        return toks[0]

    def replay(self, block):
        for e in ENGS:
            def make(e):
                def body(EE):
                    waited = {}
                    for (waits, fn, sem, amt, self_inc, eng) in self.ops[e]:
                        coll = {}
                        for (wn, wv) in waits:
                            if wv > coll.get(wn, 0):
                                coll[wn] = wv
                        own = f"c_{e}"
                        for wn, wv in coll.items():
                            if wn == "c_tensor" and e == "tensor":
                                continue
                            if wn != own and waited.get(wn, -1) >= wv:
                                continue
                            EE.wait_ge(self.sems[wn], wv)
                            waited[wn] = wv
                        if self_inc:
                            fn(EE, self.sems[sem], amt)
                        else:
                            fn(EE).then_inc(self.sems[sem], amt)
                return body
            getattr(block, e)(make(e))


def build_program():
    nc = bass.Bass("TRN2", target_bir_lowering=False)
    dr = lambda name, shape, dt, kind="ExternalInput": nc.dram_tensor(name, shape, dt, kind=kind).ap()

    x = dr("x", [S, E], F32)
    wup = dr("wup", [E, 2 * IN], F32)
    convdiag = dr("convdiag", [16, KCV, P, P], F32)
    convb2d = dr("convb2d", [P, 16], F32)
    bdq = dr("bdq", [16, P, P], F32)
    bdk = dr("bdk", [16, P, P], F32)
    bdv = dr("bdv", [16, P, P], F32)
    bdqt = dr("bdqt", [16, P, P], F32)
    bdkt = dr("bdkt", [16, P, P], F32)
    bdvt = dr("bdvt", [16, P, P], F32)
    wigfg = dr("wigfg", [3 * IN, 8], F32)
    bigfg = dr("bigfg", [8, 1], F32)
    nw2d = dr("nw2d", [P, 16], F32)
    nwrow = dr("nwrow", [1, IN], F32)
    skip2d = dr("skip2d", [P, 16], F32)
    wdn = dr("wdn", [IN, E], F32)
    wgT = dr("wgT", [E, HID], I32)
    wuT = dr("wuT", [E, HID], I32)
    wdT = dr("wdT", [HID, E], I32)
    masks = dr("masks", [4, P, 512], F32)
    ident = dr("ident", [P, P], F32)
    onesrow = dr("onesrow", [1, P], F32)
    onescol = dr("onescol", [P, 1], F32)
    epsb = dr("epsb", [P, 4], F32)

    y = dr("y", [S // 2, E], F32, kind="ExternalOutput")

    s_lnx = dr("s_lnx", [S, E], BF16, kind="Internal")
    s_kT = dr("s_kT", [IN, S], BF16, kind="Internal")
    s_qT = dr("s_qT", [IN, S], BF16, kind="Internal")
    s_vt = dr("s_vt", [S, IN], BF16, kind="Internal")
    s_xact = dr("s_xact", [IN, S], BF16, kind="Internal")
    s_swz = dr("s_swz", [IN, S], BF16, kind="Internal")
    s_hst = dr("s_hst", [IN, S], BF16, kind="Internal")
    s_y1 = dr("s_y1", [S, E], F32, kind="Internal")
    s_y1own = dr("s_y1own", [S // 2, E], F32, kind="Internal")
    s_ln2 = dr("s_ln2", [S // 2, E], BF16, kind="Internal")
    s_gat = dr("s_gat", [3, NH, S], F32, kind="Internal")
    s_wg16 = dr("s_wg16", [E, HID], BF16, kind="Internal")
    s_wu16 = dr("s_wu16", [E, HID], BF16, kind="Internal")
    s_wd16 = dr("s_wd16", [HID, E], BF16, kind="Internal")

    with contextlib.ExitStack() as top:
        em = Emitter(nc)
        em.make_sems(top)

        def SB(stack, name, shape, dt=F32, **kw):
            return stack.enter_context(nc.sbuf_tensor(name, shape, dt, **kw))

        def PS(stack, name, shape, dt=F32):
            return stack.enter_context(nc.psum_tensor(name, shape, dt))

        ones_r = SB(top, "ones_r", [1, P])
        ones_cf = SB(top, "ones_cf", [P, 1])
        ones_cb = SB(top, "ones_cb", [P, 1], BF16)
        ident_s = SB(top, "ident_s", [P, P])
        nw2d_s = SB(top, "nw2d_s", [P, 16])
        skip2d_s = SB(top, "skip2d_s", [P, 16])
        bigfg_s = SB(top, "bigfg_s", [8, 1])
        nwrow_s = SB(top, "nwrow_s", [1, IN])
        masks_s = SB(top, "masks_s", [P, 4 * 512])

        epsb_s = SB(top, "epsb_s", [P, 4])
        em.dma("sync", epsb_s[:], epsb, w=["epsb_s"])
        em.dma("sync", ones_r[:], onesrow, w=["ones_r"])
        em.dma("sync", ones_cf[:], onescol, w=["ones_cf"])
        em.dma("sync", ident_s[:], ident, w=["ident_s"])
        em.dma("sync", nw2d_s[:], nw2d, w=["nw2d_s"])
        em.dma("sync", skip2d_s[:], skip2d, w=["skip2d_s"])
        em.dma("sync", bigfg_s[:], bigfg, w=["bigfg_s"])
        em.dma("sync", nwrow_s[:], nwrow, w=["nwrow_s"])
        em.dma("sync", masks_s[:].rearrange("p (r n) -> p r n", r=4), masks.rearrange("r p n -> p r n"), w=["masks_s"])
        em.op("scalar", lambda EE: EE.activation(ones_cb[:], ones_cf[:], AF.Copy),
              r=["ones_cf"], w=["ones_cb"])
        gacc = [SB(top, f"gacc{j}", [8, S]) for j in range(2)]

        with contextlib.ExitStack() as ph02:
            convb_s = SB(ph02, "convb_s", [P, 16])
            cd_b = SB(ph02, "cd_b", [P, 16 * KCV * P], BF16)
            bdq_b = SB(ph02, "bdq_b", [P, 16 * P], BF16)
            bdk_b = SB(ph02, "bdk_b", [P, 16 * P], BF16)
            bdv_b = SB(ph02, "bdv_b", [P, 16 * P], BF16)
            cqk_b = SB(ph02, "cqk_b", [P, 16 * 8], BF16)
            cv_b = SB(ph02, "cv_b", [P, 16 * 8], BF16)

            em.dma("sync", convb_s[:], convb2d, w=["convb_s"])
            with contextlib.ExitStack() as ph0:
                cstg = SB(ph0, "cstg", [P, 4 * P])
                bstg = SB(ph0, "bstg", [P, 3 * P])
                wig_s = SB(ph0, "wig_s", [P, 48 * 8])
                btstg = SB(ph0, "btstg", [P, 3 * P])
                ps_cq = PS(ph0, "ps_cq", [P, 8])
                ps_cv = PS(ph0, "ps_cv", [P, 8])
                for c in range(16):
                    em.dma("sync", cstg[:].rearrange("p (i q) -> p i q", i=4), convdiag[c].rearrange("i p q -> p i q"),
                           w=["cstg"])
                    em.op("scalar", lambda EE, c=c: EE.activation(
                        cd_b[:, c * 4 * P:(c + 1) * 4 * P], cstg[:], AF.Copy),
                        r=["cstg"], w=["cd_b"])
                for c in range(16):
                    em.dma("sync", bstg[:, 0:P], bdq[c], w=["bstg0"], r=[])
                    em.dma("sync", bstg[:, P:2 * P], bdk[c], w=["bstg1"])
                    em.dma("sync", bstg[:, 2 * P:3 * P], bdv[c], w=["bstg2"])
                    em.op("scalar", lambda EE, c=c: EE.activation(
                        bdq_b[:, c * P:(c + 1) * P], bstg[:, 0:P], AF.Copy),
                        r=["bstg0"], w=["bdq_b"])
                    em.op("scalar", lambda EE, c=c: EE.activation(
                        bdk_b[:, c * P:(c + 1) * P], bstg[:, P:2 * P], AF.Copy),
                        r=["bstg1"], w=["bdk_b"])
                    em.op("scalar", lambda EE, c=c: EE.activation(
                        bdv_b[:, c * P:(c + 1) * P], bstg[:, 2 * P:3 * P], AF.Copy),
                        r=["bstg2"], w=["bdv_b"])
                em.dma("sync", wig_s[:].rearrange("p (t g) -> p t g", t=48), wigfg.rearrange("(t p) g -> p t g", p=P),
                       w=["wig_s"])
                for c in range(16):
                    em.dma("sync", btstg[:, 0:P], bdqt[c], w=["btstg0"])
                    em.dma("sync", btstg[:, P:2 * P], bdkt[c], w=["btstg1"])
                    em.dma("sync", btstg[:, 2 * P:3 * P], bdvt[c], w=["btstg2"])
                    em.op("tensor", lambda EE, c=c: EE.matmul(
                        ps_cq[:], btstg[:, 0:P], wig_s[:, 8 * c:8 * c + 8],
                        start=True, stop=False), r=["btstg0", "wig_s"], w=["ps_cq"])
                    em.op("tensor", lambda EE, c=c: EE.matmul(
                        ps_cq[:], btstg[:, P:2 * P], wig_s[:, 8 * (16 + c):8 * (16 + c) + 8],
                        start=False, stop=True), r=["btstg1", "wig_s"], w=["ps_cq"])
                    em.op("tensor", lambda EE, c=c: EE.matmul(
                        ps_cv[:], btstg[:, 2 * P:3 * P], wig_s[:, 8 * (32 + c):8 * (32 + c) + 8],
                        start=True, stop=True), r=["btstg2", "wig_s"], w=["ps_cv"])
                    em.op("vector", lambda EE, c=c: EE.tensor_copy(
                        cqk_b[:, 8 * c:8 * c + 8], ps_cq[:]), r=["ps_cq"], w=["cqk_b"])
                    em.op("vector", lambda EE, c=c: EE.tensor_copy(
                        cv_b[:, 8 * c:8 * c + 8], ps_cv[:]), r=["ps_cv"], w=["cv_b"])
                em.barrier()

            with contextlib.ExitStack() as ph1:
                xt_b = [SB(ph1, f"xt{i}", [P, E]) for i in range(2)]
                xsq_b = [SB(ph1, f"xsq{i}", [P, E]) for i in range(2)]
                lnx_b = [SB(ph1, f"lnxo{i}", [P, E], BF16) for i in range(2)]
                st_a = SB(ph1, "st_a", [P, 8])
                for t in range(32):
                    i = t % 2
                    em.dma("sync", xt_b[i][:], x[P * t:P * (t + 1), :], w=[f"xt{i}"])
                    em.op("vector", lambda EE, i=i: EE.reduce_sum(st_a[:, 0:1], xt_b[i][:], AX.X),
                          r=[f"xt{i}"], w=["sa0"])
                    em.op("scalar", lambda EE, i=i: EE.activation(xsq_b[i][:], xt_b[i][:], AF.Square),
                          r=[f"xt{i}"], w=[f"xsq{i}"])
                    em.op("vector", lambda EE, i=i: EE.reduce_sum(st_a[:, 1:2], xsq_b[i][:], AX.X),
                          r=[f"xsq{i}"], w=["sa1"])
                    em.op("vector", lambda EE: EE.tensor_scalar(st_a[:, 2:3], st_a[:, 0:1], 1.0 / E, None, ALU.mult),
                          r=["sa0"], w=["sa2"])
                    em.op("vector", lambda EE: EE.tensor_scalar(st_a[:, 3:4], st_a[:, 1:2], 1.0 / E, None, ALU.mult),
                          r=["sa1"], w=["sa3"])
                    em.op("vector", lambda EE: EE.tensor_tensor(st_a[:, 4:5], st_a[:, 2:3], st_a[:, 2:3], ALU.mult),
                          r=["sa2"], w=["sa4"])
                    em.op("vector", lambda EE: EE.tensor_tensor(st_a[:, 5:6], st_a[:, 3:4], st_a[:, 4:5], ALU.subtract),
                          r=["sa3", "sa4"], w=["sa5"])
                    em.op("scalar", lambda EE: EE.activation(st_a[:, 6:7], st_a[:, 5:6], AF.Sqrt, bias=epsb_s[:, 0:1]),
                          r=["sa5", "epsb_s"], w=["sa6"])
                    em.op("vector", lambda EE: EE.reciprocal(st_a[:, 7:8], st_a[:, 6:7]),
                          r=["sa6"], w=["sa7"])
                    em.op("vector", lambda EE, i=i: EE.tensor_scalar(
                        lnx_b[i][:], xt_b[i][:], st_a[:, 2:3], st_a[:, 7:8], ALU.subtract, ALU.mult),
                        r=[f"xt{i}", "sa2", "sa7"], w=[f"lnxo{i}"])
                    em.dma("gpsimd", s_lnx[P * t:P * (t + 1), :], lnx_b[i][:],
                           r=[f"lnxo{i}"], w=["s_lnx"])
                em.barrier()

            lnxT = [SB(ph02, f"lnxT{c}", [P, S], BF16, is_transpose_dest=True) for c in range(8)]
            for c in range(8):
                em.dma_op("sync", lambda EE, c=c: EE.dma_start_transpose(
                    lnxT[c][:], s_lnx[:, P * c:P * (c + 1)]), r=["s_lnx"], w=[f"lnxT{c}"])

            with contextlib.ExitStack() as ph2:
                wf_b = [[SB(ph2, f"wf{j}_{k}", [P, P]) for k in range(8)] for j in range(2)]
                wb_b = [[SB(ph2, f"wb{j}_{k}", [P, P], BF16) for k in range(8)] for j in range(2)]
                xmb_b = [SB(ph2, f"xmb{j}", [P, 4 + S], BF16) for j in range(2)]
                xab_b = [SB(ph2, f"xab{j}", [P, S], BF16) for j in range(2)]
                kstg = [SB(ph2, f"kstg{j}", [P, 512], BF16) for j in range(2)]
                qstg = [SB(ph2, f"qstg{j}", [P, 512], BF16) for j in range(2)]
                vstg = [SB(ph2, f"vstg{j}", [P, 512], BF16) for j in range(2)]
                ps_w = [PS(ph2, f"ps_w{j}", [P, 512]) for j in range(2)]
                ps_c = [PS(ph2, f"ps_c{j}", [P, 512]) for j in range(2)]
                ps_k = [PS(ph2, f"ps_k{j}", [P, 512]) for j in range(2)]
                ps_g = PS(ph2, "ps_g", [8, 512])

                for c in range(32):
                    j = c % 2
                    mlstm = c < 16
                    for k in range(8):
                        em.dma("sync", wf_b[j][k][:], wup[P * k:P * (k + 1), P * c:P * (c + 1)],
                               w=[f"wf{j}_{k}"])
                    for k in range(8):
                        em.op("scalar", lambda EE, j=j, k=k: EE.activation(
                            wb_b[j][k][:], wf_b[j][k][:], AF.Copy),
                            r=[f"wf{j}_{k}"], w=[f"wb{j}_{k}"])
                    if mlstm:
                        em.op("vector", lambda EE, j=j: EE.memset(xmb_b[j][:, 0:4], 0.0),
                              w=[f"xmbz{j}"], r=[f"xmb{j}"])
                    for t in range(8):
                        jj = t % 2
                        for k in range(8):
                            em.op("tensor", lambda EE, j=j, k=k, t=t, jj=jj: EE.matmul(
                                ps_w[jj][:], wb_b[j][k][:], lnxT[k][:, 512 * t:512 * (t + 1)],
                                start=(k == 0), stop=(k == 7)),
                                r=[f"wb{j}_{k}", f"lnxT{k}"], w=[f"ps_w{jj}"])
                        if mlstm:
                            em.op("scalar", lambda EE, j=j, t=t, jj=jj: EE.activation(
                                xmb_b[j][:, 4 + 512 * t:4 + 512 * (t + 1)], ps_w[jj][:], AF.Copy),
                                r=[f"ps_w{jj}"], w=[f"xmb{j}"])
                        else:
                            em.op("scalar", lambda EE, j=j, t=t, jj=jj: EE.activation(
                                xmb_b[j][:, 4 + 512 * t:4 + 512 * (t + 1)], ps_w[jj][:], AF.Silu),
                                r=[f"ps_w{jj}"], w=[f"xmb{j}"])
                    if not mlstm:
                        em.dma("gpsimd", s_swz[P * (c - 16):P * (c - 15), :], xmb_b[j][:, 4:4 + S],
                               r=[f"xmb{j}"], w=["s_swz"])
                        continue

                    for t in range(8):
                        jj = t % 2
                        for i in range(KCV):
                            em.op("tensor", lambda EE, j=j, c=c, i=i, t=t, jj=jj: EE.matmul(
                                ps_c[jj][:], cd_b[:, (c * 4 + i) * P:(c * 4 + i + 1) * P],
                                xmb_b[j][:, 512 * t + 1 + i:512 * t + 1 + i + 512],
                                start=(i == 0), stop=(i == KCV - 1)),
                                r=["cd_b", f"xmb{j}", f"xmbz{j}"], w=[f"ps_c{jj}"])
                        em.op("scalar", lambda EE, j=j, c=c, t=t, jj=jj: EE.activation(
                            xab_b[j][:, 512 * t:512 * (t + 1)], ps_c[jj][:], AF.Silu,
                            bias=convb_s[:, c:c + 1]),
                            r=[f"ps_c{jj}", "convb_s"], w=[f"xab{j}"])
                    em.dma("gpsimd", s_xact[P * c:P * (c + 1), :], xab_b[j][:],
                           r=[f"xab{j}"], w=["s_xact"])

                    for t in range(8):
                        jj = t % 2
                        em.op("tensor", lambda EE, j=j, c=c, t=t, jj=jj: EE.matmul(
                            ps_k[jj][:], bdk_b[:, c * P:(c + 1) * P], xab_b[j][:, 512 * t:512 * (t + 1)],
                            start=True, stop=True),
                            r=["bdk_b", f"xab{j}"], w=[f"ps_k{jj}"])
                        em.op("vector", lambda EE, jj=jj: EE.tensor_copy(kstg[jj][:], ps_k[jj][:]),
                              r=[f"ps_k{jj}"], w=[f"kstg{jj}"])
                        em.dma("gpsimd", s_kT[P * c:P * (c + 1), 512 * t:512 * (t + 1)], kstg[jj][:],
                               r=[f"kstg{jj}"], w=["s_kT"])
                        em.op("tensor", lambda EE, j=j, c=c, t=t, jj=jj: EE.matmul(
                            ps_k[jj][:], bdq_b[:, c * P:(c + 1) * P], xab_b[j][:, 512 * t:512 * (t + 1)],
                            start=True, stop=True),
                            r=["bdq_b", f"xab{j}"], w=[f"ps_k{jj}"])
                        em.op("vector", lambda EE, jj=jj: EE.tensor_scalar(
                            qstg[jj][:], ps_k[jj][:], SC, None, ALU.mult),
                            r=[f"ps_k{jj}"], w=[f"qstg{jj}"])
                        em.dma("gpsimd", s_qT[P * c:P * (c + 1), 512 * t:512 * (t + 1)], qstg[jj][:],
                               r=[f"qstg{jj}"], w=["s_qT"])

                    for g in range(8):
                        jj = g % 2
                        for qq in range(4):
                            tt = 4 * g + qq
                            em.op("tensor", lambda EE, j=j, c=c, tt=tt, jj=jj, qq=qq: EE.matmul(
                                ps_c[jj][:, P * qq:P * (qq + 1)],
                                xmb_b[j][:, 4 + P * tt:4 + P * (tt + 1)], bdv_b[:, c * P:(c + 1) * P],
                                start=True, stop=True),
                                r=["bdv_b", f"xmb{j}"], w=[f"ps_c{jj}"])
                        em.op("scalar", lambda EE, jj=jj: EE.activation(vstg[jj][:], ps_c[jj][:], AF.Copy),
                              r=[f"ps_c{jj}"], w=[f"vstg{jj}"])
                        for qq in range(4):
                            tt = 4 * g + qq
                            em.dma("gpsimd", s_vt[P * tt:P * (tt + 1), P * c:P * (c + 1)],
                                   vstg[jj][:, P * qq:P * (qq + 1)],
                                   r=[f"vstg{jj}"], w=["s_vt"])

                    for t in range(8):
                        em.op("tensor", lambda EE, c=c, t=t, j=j: EE.matmul(
                            ps_g[:], cqk_b[:, 8 * c:8 * (c + 1)], xab_b[j][:, 512 * t:512 * (t + 1)],
                            start=True, stop=False),
                            r=["cqk_b", f"xab{j}"], w=["ps_g"])
                        em.op("tensor", lambda EE, c=c, t=t, j=j: EE.matmul(
                            ps_g[:], cv_b[:, 8 * c:8 * (c + 1)], xmb_b[j][:, 4 + 512 * t:4 + 512 * (t + 1)],
                            start=False, stop=True),
                            r=["cv_b", f"xmb{j}"], w=["ps_g"])
                        if c == 0:
                            em.op("vector", lambda EE, t=t: EE.tensor_copy(
                                gacc[1][:, 512 * t:512 * (t + 1)], ps_g[:]),
                                r=["ps_g"], w=[f"ga1_{t}"])
                        else:
                            sj, dj = (1, 0) if c % 2 == 1 else (0, 1)
                            em.op("vector", lambda EE, t=t, sj=sj, dj=dj: EE.tensor_tensor(
                                gacc[dj][:, 512 * t:512 * (t + 1)],
                                gacc[sj][:, 512 * t:512 * (t + 1)], ps_g[:], ALU.add),
                                r=["ps_g", f"ga{sj}_{t}"], w=[f"ga{dj}_{t}"])
                em.barrier()

        with contextlib.ExitStack() as ph3:
            g2 = SB(ph3, "g2", [8, S])
            gf = SB(ph3, "gf", [4, S])
            gwk = SB(ph3, "gwk", [4, S])
            gwk2 = SB(ph3, "gwk2", [4, S])
            lfc = SB(ph3, "lfc", [4, S])
            av = SB(ph3, "av", [4, S])
            mv = SB(ph3, "mv", [4, S])
            zz4 = SB(ph3, "zz4", [4, S])
            gfin = gacc[0]
            em.op("vector", lambda EE: EE.memset(zz4[:], 0.0), w=["zz4"])
            em.op("vector", lambda EE: EE.tensor_scalar(
                g2[:], gfin[:], bigfg_s[:], None, ALU.add),
                r=["bigfg_s"] + [f"ga0_{t}" for t in range(8)], w=["g2"])
            em.dma("sync", gf[:], g2[4:8, :], r=["g2"], w=["gf"])
            em.op("scalar", lambda EE: EE.activation(gwk[:], gf[:], AF.Abs),
                  r=["gf"], w=["gwk"])
            em.op("scalar", lambda EE: EE.activation(gwk2[:], gwk[:], AF.Exp, scale=-1.0),
                  r=["gwk"], w=["gwk2"])
            em.op("scalar", lambda EE: EE.activation(gwk[:], gwk2[:], AF.Ln, bias=epsb_s[0:4, 2:3]),
                  r=["gwk2", "epsb_s"], w=["gwk"])
            em.op("vector", lambda EE: EE.tensor_scalar(gwk2[:], gf[:], 0.0, None, ALU.min),
                  r=["gf"], w=["gwk2"])
            em.op("vector", lambda EE: EE.tensor_tensor(gf[:], gwk2[:], gwk[:], ALU.subtract),
                  r=["gwk", "gwk2"], w=["gf"])
            em.op("vector", lambda EE: EE.tensor_tensor_scan(
                lfc[:], gf[:], zz4[:], 0.0, ALU.add, ALU.add),
                r=["gf", "zz4"], w=["lfc"])
            em.op("vector", lambda EE: EE.tensor_tensor(av[:], g2[0:4, :], lfc[:], ALU.subtract),
                  r=["g2", "lfc"], w=["av"])
            em.op("vector", lambda EE: EE.tensor_tensor_scan(
                mv[:], av[:], av[:], -1e30, ALU.max, ALU.max),
                r=["av"], w=["mv"])
            em.op("vector", lambda EE: EE.tensor_tensor(gwk[:], lfc[:], mv[:], ALU.add),
                  r=["lfc", "mv"], w=["gwk"])
            em.op("scalar", lambda EE: EE.activation(gwk2[:], gwk[:], AF.Exp, scale=-1.0),
                  r=["gwk"], w=["gwk2"])
            em.dma("gpsimd", s_gat[0], av[:], r=["av"], w=["s_gat"])
            em.dma("gpsimd", s_gat[1], mv[:], r=["mv"], w=["s_gat"])
            em.dma("gpsimd", s_gat[2], gwk2[:], r=["gwk2"], w=["s_gat"])
            em.barrier()

        with contextlib.ExitStack() as ph4:
            ktb = [SB(ph4, f"ktb{d}", [P, S], BF16) for d in range(4)]
            vtb = SB(ph4, "vtb", [P, 32 * 512], BF16)
            qtb = [SB(ph4, f"qtb{d}", [P, 512], BF16) for d in range(4)]
            arow = SB(ph4, "arow", [32, P])
            aT = SB(ph4, "aT", [P, 32])
            mfb = SB(ph4, "mfb", [1, S])
            nmfb = SB(ph4, "nmfb", [1, S])
            mbs = SB(ph4, "mbs", [P, 512])
            mbm = SB(ph4, "mbm", [P, 512])
            ee_b = [SB(ph4, f"ee{i}", [P, 512]) for i in range(3)]
            ce_b = [SB(ph4, f"ce{i}", [P, 512], BF16) for i in range(3)]
            hf_b = [SB(ph4, f"hf{d}", [P, 512]) for d in range(4)]
            hq_b = [SB(ph4, f"hq{d}", [P, 512]) for d in range(4)]
            e1 = SB(ph4, "e1", [1, 512 * 8])
            anw_s = SB(ph4, "anw_s", [P, 512])
            xas = [SB(ph4, f"xas{i}", [P, 512], BF16) for i in range(2)]
            szs = [SB(ph4, f"szs{i}", [P, 512], BF16) for i in range(2)]
            hst_stg = [SB(ph4, f"hstg{i}", [P, 512], BF16) for i in range(2)]
            ps_ct = [PS(ph4, f"ps_ct{i}", [P, 512]) for i in range(3)]
            ps_h = [PS(ph4, f"ps_h{d}", [P, 512]) for d in range(4)]
            ps_x = PS(ph4, "ps_x", [P, 512])
            ps_r = ps_x

            for h in range(NH):
                for d in range(4):
                    em.dma("sync", ktb[d][:], s_kT[DH * h + P * d:DH * h + P * (d + 1), :],
                           r=["s_kT"], w=[f"ktb{d}"])
                for jt in range(32):
                    em.dma("sync", vtb[:, 512 * jt:512 * (jt + 1)],
                           s_vt[P * jt:P * (jt + 1), DH * h:DH * (h + 1)],
                           r=["s_vt"], w=["vtb"])
                em.dma("sync", arow[:], s_gat[0, h].rearrange("(j n) -> j n", j=32),
                       r=["s_gat"], w=["arow"])
                em.op("tensor", lambda EE: EE.transpose(ps_x[:, 0:32], arow[:], ident_s[0:32, 0:32]),
                      r=["arow", "ident_s"], w=["ps_x"])
                em.op("vector", lambda EE: EE.tensor_copy(aT[:], ps_x[:, 0:32]),
                      r=["ps_x"], w=["aT"])
                em.dma("sync", mfb[:], s_gat[1, h].rearrange("(o n) -> o n", o=1), r=["s_gat"], w=["mfb"])
                em.dma("sync", nmfb[:], s_gat[2, h].rearrange("(o n) -> o n", o=1), r=["s_gat"], w=["nmfb"])

                for qi in range(8):
                    for d in range(4):
                        em.dma("sync", qtb[d][:],
                               s_qT[DH * h + P * d:DH * h + P * (d + 1), 512 * qi:512 * (qi + 1)],
                               r=["s_qT"], w=[f"qtb{d}"])
                    em.op("tensor", lambda EE, qi=qi: EE.matmul(
                        ps_x[:], ones_r[:], mfb[0:1, 512 * qi:512 * (qi + 1)],
                        start=True, stop=True), r=["ones_r", "mfb"], w=["ps_x"])
                    em.op("vector", lambda EE: EE.tensor_copy(mbs[:], ps_x[:]),
                          r=["ps_x"], w=["mbs"])
                    nkj = 4 * qi + 4

                    def emit_c(kj):
                        i = kj % 3
                        for d in range(4):
                            em.op("tensor", lambda EE, kj=kj, d=d, i=i: EE.matmul(
                                ps_ct[i][:], ktb[d][:, P * kj:P * (kj + 1)], qtb[d][:],
                                start=(d == 0), stop=(d == 3)),
                                r=[f"ktb{d}", f"qtb{d}"], w=[f"ps_ct{i}"])
                    emit_c(0)
                    if nkj > 1:
                        emit_c(1)
                    for kj in range(nkj):
                        i = kj % 3
                        rdiag = kj - 4 * qi
                        if kj + 2 < nkj:
                            emit_c(kj + 2)
                        if rdiag >= 0:
                            em.op("vector", lambda EE, rdiag=rdiag: EE.tensor_tensor(
                                mbm[:], mbs[:], masks_s[:, 512 * rdiag:512 * (rdiag + 1)], ALU.add),
                                r=["mbs", "masks_s"], w=["mbm"])
                            em.op("scalar", lambda EE, kj=kj, i=i: EE.activation(
                                ee_b[i][:], mbm[:], AF.Exp, bias=aT[:, kj:kj + 1], scale=-1.0),
                                r=["mbm", "aT"], w=[f"ee{i}"])
                        else:
                            em.op("scalar", lambda EE, kj=kj, i=i: EE.activation(
                                ee_b[i][:], mbs[:], AF.Exp, bias=aT[:, kj:kj + 1], scale=-1.0),
                                r=["mbs", "aT"], w=[f"ee{i}"])
                        em.op("vector", lambda EE, i=i: EE.tensor_tensor(
                            ce_b[i][:], ps_ct[i][:], ee_b[i][:], ALU.mult),
                            r=[f"ps_ct{i}", f"ee{i}"], w=[f"ce{i}"])
                        for d in range(4):
                            em.op("tensor", lambda EE, kj=kj, d=d, i=i, nkj=nkj: EE.matmul(
                                ps_h[d][:], vtb[:, 512 * kj + P * d:512 * kj + P * (d + 1)], ce_b[i][:],
                                start=(kj == 0), stop=(kj == nkj - 1)),
                                r=["vtb", f"ce{i}"], w=[f"ps_h{d}"])
                        em.op("tensor", lambda EE, kj=kj, i=i, nkj=nkj: EE.matmul(
                            ps_r[0:1, :], ones_cb[:], ce_b[i][:],
                            start=(kj == 0), stop=(kj == nkj - 1)),
                            r=["ones_cb", f"ce{i}"], w=["ps_x"])

                    em.op("scalar", lambda EE: EE.activation(
                        e1[0:1, 0:512], ps_r[0:1, :], AF.Abs),
                        r=["ps_x"], w=["e1A"])
                    em.op("vector", lambda EE, qi=qi: EE.tensor_tensor(
                        e1[0:1, 512:1024], e1[0:1, 0:512],
                        nmfb[0:1, 512 * qi:512 * (qi + 1)], ALU.max),
                        r=["e1A", "nmfb"], w=["e1B"])
                    em.op("vector", lambda EE: EE.tensor_scalar(
                        e1[0:1, 1024:1536], e1[0:1, 512:1024], 1e-6, None, ALU.add),
                        r=["e1B"], w=["e1C"])
                    em.op("vector", lambda EE: EE.reciprocal(e1[0:1, 1536:2048], e1[0:1, 1024:1536]),
                          r=["e1C"], w=["e1D"])
                    em.op("tensor", lambda EE: EE.matmul(
                        ps_x[:], ones_r[:], e1[0:1, 1536:2048], start=True, stop=True),
                        r=["ones_r", "e1D"], w=["ps_x"])
                    em.op("vector", lambda EE: EE.tensor_copy(mbs[:], ps_x[:]),
                          r=["ps_x"], w=["mbs"])
                    for d in range(4):
                        em.op("vector", lambda EE, d=d: EE.tensor_tensor(
                            hf_b[d][:], ps_h[d][:], mbs[:], ALU.mult),
                            r=[f"ps_h{d}", "mbs"], w=[f"hf{d}"])
                        em.op("scalar", lambda EE, d=d: EE.activation(
                            hq_b[d][:], hf_b[d][:], AF.Square),
                            r=[f"hf{d}"], w=[f"hq{d}"])
                    for d in range(4):
                        em.op("tensor", lambda EE, d=d: EE.matmul(
                            ps_r[0:1, :], ones_cf[:], hf_b[d][:], start=(d == 0), stop=(d == 3)),
                            r=["ones_cf", f"hf{d}"], w=["ps_x"])
                    em.op("vector", lambda EE: EE.tensor_scalar(
                        e1[0:1, 2048:2560], ps_r[0:1, :], 1.0 / DH, None, ALU.mult),
                        r=["ps_x"], w=["e1MU"])
                    for d in range(4):
                        em.op("tensor", lambda EE, d=d: EE.matmul(
                            ps_r[0:1, :], ones_cf[:], hq_b[d][:], start=(d == 0), stop=(d == 3)),
                            r=["ones_cf", f"hq{d}"], w=["ps_x"])
                    em.op("vector", lambda EE: EE.tensor_scalar(
                        e1[0:1, 2560:3072], ps_r[0:1, :], 1.0 / DH, None, ALU.mult),
                        r=["ps_x"], w=["e1MS"])
                    em.op("vector", lambda EE: EE.tensor_tensor(
                        e1[0:1, 3072:3584], e1[0:1, 2048:2560], e1[0:1, 2048:2560], ALU.mult),
                        r=["e1MU"], w=["e1M2"])
                    em.op("vector", lambda EE: EE.tensor_tensor(
                        e1[0:1, 3584:4096], e1[0:1, 2560:3072], e1[0:1, 3072:3584], ALU.subtract),
                        r=["e1MS", "e1M2"], w=["e1VA"])
                    em.op("scalar", lambda EE: EE.activation(
                        e1[0:1, 0:512], e1[0:1, 3584:4096], AF.Sqrt, bias=epsb_s[0:1, 1:2]),
                        r=["e1VA", "epsb_s"], w=["e1A"])
                    em.op("vector", lambda EE: EE.reciprocal(e1[0:1, 512:1024], e1[0:1, 0:512]),
                          r=["e1A"], w=["e1B"])
                    em.op("vector", lambda EE: EE.tensor_tensor(
                        e1[0:1, 1024:1536], e1[0:1, 2048:2560], e1[0:1, 512:1024], ALU.mult),
                        r=["e1MU", "e1B"], w=["e1C"])
                    for d in range(4):
                        ch = 4 * h + d
                        em.op("tensor", lambda EE, ch=ch: EE.matmul(
                            ps_x[:], nwrow_s[0:1, P * ch:P * (ch + 1)], e1[0:1, 512:1024],
                            start=True, stop=True), r=["nwrow_s", "e1B"], w=["ps_x"])
                        em.op("vector", lambda EE: EE.tensor_copy(anw_s[:], ps_x[:]),
                              r=["ps_x"], w=["anw_s"])
                        em.op("tensor", lambda EE, ch=ch: EE.matmul(
                            ps_x[:], nwrow_s[0:1, P * ch:P * (ch + 1)], e1[0:1, 1024:1536],
                            start=True, stop=True), r=["nwrow_s", "e1C", "anw_s"], w=["ps_x"])
                        i2 = d % 2
                        em.dma("sync", xas[i2][:], s_xact[P * ch:P * (ch + 1), 512 * qi:512 * (qi + 1)],
                               r=["s_xact"], w=[f"xas{i2}"])
                        em.dma("sync", szs[i2][:], s_swz[P * ch:P * (ch + 1), 512 * qi:512 * (qi + 1)],
                               r=["s_swz"], w=[f"szs{i2}"])
                        em.op("vector", lambda EE, ch=ch, i2=i2: EE.scalar_tensor_tensor(
                            ee_b[0][:], xas[i2][:], skip2d_s[:, ch:ch + 1], ps_x[:],
                            ALU.mult, ALU.subtract),
                            r=[f"xas{i2}", "skip2d_s", "ps_x"], w=["ee0"])
                        em.op("vector", lambda EE, d=d: EE.tensor_tensor(
                            ee_b[1][:], hf_b[d][:], anw_s[:], ALU.mult),
                            r=[f"hf{d}", "anw_s"], w=["ee1"])
                        em.op("vector", lambda EE: EE.tensor_tensor(
                            mbm[:], ee_b[1][:], ee_b[0][:], ALU.add),
                            r=["ee0", "ee1"], w=["mbm"])
                        em.op("vector", lambda EE, i2=i2: EE.tensor_tensor(
                            hst_stg[i2][:], mbm[:], szs[i2][:], ALU.mult),
                            r=["mbm", f"szs{i2}"], w=[f"hstg{i2}"])
                        em.dma("gpsimd", s_hst[P * ch:P * (ch + 1), 512 * qi:512 * (qi + 1)],
                               hst_stg[i2][:], r=[f"hstg{i2}"], w=["s_hst"])
            em.barrier()

        with contextlib.ExitStack() as ph5:
            wdf = [SB(ph5, f"wdf{i}", [P, E]) for i in range(2)]
            wdb = [SB(ph5, f"wdb{c}", [P, E], BF16) for c in range(16)]
            hs = [SB(ph5, f"hs{c}", [P, IN], BF16) for c in range(16)]
            ao = [SB(ph5, f"ao{i}", [P, E]) for i in range(2)]
            aq = [SB(ph5, f"aq{i}", [P, E]) for i in range(2)]
            xr = [SB(ph5, f"xr{i}", [P, E]) for i in range(2)]
            y1s = [SB(ph5, f"y1s{i}", [P, E]) for i in range(2)]
            st5 = SB(ph5, "st5", [P, 8])
            ps_d = [PS(ph5, f"ps_d{i}", [P, 512]) for i in range(4)]

            for c in range(16):
                i = c % 2
                em.dma("sync", wdf[i][:], wdn[P * c:P * (c + 1), :], w=[f"wdf{i}"])
                em.op("scalar", lambda EE, c=c, i=i: EE.activation(wdb[c][:], wdf[i][:], AF.Copy),
                      r=[f"wdf{i}"], w=[f"wdb{c}"])
            for half in range(2):
                for c in range(16):
                    em.dma("sync", hs[c][:], s_hst[P * c:P * (c + 1), IN * half:IN * (half + 1)],
                           r=["s_hst"], w=[f"hs{c}"])
                for tt in range(16):
                    i = tt % 2
                    for c in range(16):
                        em.op("tensor", lambda EE, c=c, tt=tt, i=i: EE.matmul(
                            ps_d[2 * i][:], hs[c][:, P * tt:P * (tt + 1)], wdb[c][:, 0:512],
                            start=(c == 0), stop=(c == 15)),
                            r=[f"hs{c}", f"wdb{c}"], w=[f"ps_d{2 * i}"])
                        em.op("tensor", lambda EE, c=c, tt=tt, i=i: EE.matmul(
                            ps_d[2 * i + 1][:], hs[c][:, P * tt:P * (tt + 1)], wdb[c][:, 512:1024],
                            start=(c == 0), stop=(c == 15)),
                            r=[f"hs{c}", f"wdb{c}"], w=[f"ps_d{2 * i + 1}"])
                    em.op("vector", lambda EE, i=i: EE.tensor_copy(ao[i][:, 0:512], ps_d[2 * i][:]),
                          r=[f"ps_d{2 * i}"], w=[f"ao{i}"])
                    em.op("vector", lambda EE, i=i: EE.tensor_copy(ao[i][:, 512:1024], ps_d[2 * i + 1][:]),
                          r=[f"ps_d{2 * i + 1}"], w=[f"ao{i}"])
                    row = P * (16 * half + tt)
                    em.dma("sync", xr[i][:], x[row:row + P, :], w=[f"xr{i}"])
                    em.op("vector", lambda EE, i=i: EE.reduce_sum(st5[:, 0:1], ao[i][:], AX.X),
                          r=[f"ao{i}"], w=["sb0"])
                    em.op("scalar", lambda EE, i=i: EE.activation(aq[i][:], ao[i][:], AF.Square),
                          r=[f"ao{i}"], w=[f"aq{i}"])
                    em.op("vector", lambda EE, i=i: EE.reduce_sum(st5[:, 1:2], aq[i][:], AX.X),
                          r=[f"aq{i}"], w=["sb1"])
                    em.op("vector", lambda EE: EE.tensor_scalar(st5[:, 2:3], st5[:, 0:1], 1.0 / E, None, ALU.mult),
                          r=["sb0"], w=["sb2"])
                    em.op("vector", lambda EE: EE.tensor_scalar(st5[:, 3:4], st5[:, 1:2], 1.0 / E, None, ALU.mult),
                          r=["sb1"], w=["sb3"])
                    em.op("vector", lambda EE: EE.tensor_tensor(st5[:, 4:5], st5[:, 2:3], st5[:, 2:3], ALU.mult),
                          r=["sb2"], w=["sb4"])
                    em.op("vector", lambda EE: EE.tensor_tensor(st5[:, 5:6], st5[:, 3:4], st5[:, 4:5], ALU.subtract),
                          r=["sb3", "sb4"], w=["sb5"])
                    em.op("scalar", lambda EE: EE.activation(st5[:, 6:7], st5[:, 5:6], AF.Sqrt, bias=epsb_s[:, 0:1]),
                          r=["sb5", "epsb_s"], w=["sb6"])
                    em.op("vector", lambda EE: EE.reciprocal(st5[:, 7:8], st5[:, 6:7]),
                          r=["sb6"], w=["sb7"])
                    em.op("vector", lambda EE, i=i: EE.tensor_scalar(
                        ao[i][:], ao[i][:], st5[:, 2:3], st5[:, 7:8], ALU.subtract, ALU.mult),
                        r=["sb2", "sb7"], w=[f"ao{i}"])
                    em.op("vector", lambda EE, i=i: EE.tensor_tensor(
                        y1s[i][:], ao[i][:], xr[i][:], ALU.add),
                        r=[f"ao{i}", f"xr{i}"], w=[f"y1s{i}"])
                    em.dma("gpsimd", s_y1[row:row + P, :], y1s[i][:],
                           r=[f"y1s{i}"], w=["s_y1"])
            em.barrier()

        def sw_fn(EE, sem, amt):
            for arm in nc.switch_core_id(engines=[EE], n=128):
                try:
                    hg = arm.logical % 2
                except Exception:
                    hg = int(arm) % 2
                EE.dma_start(s_y1own, s_y1[2048 * hg:2048 * (hg + 1), :]).then_inc(sem, amt)
        em.dma_op("sync", sw_fn, r=["s_y1"], w=["s_y1own"], self_inc=True)
        em.barrier()

        with contextlib.ExitStack() as ph6w:
            wst = [SB(ph6w, f"wst{i}", [P, HID], I32) for i in range(2)]
            wsb = [SB(ph6w, f"wsb{i}", [P, HID], BF16) for i in range(2)]
            for idx, (src, dst) in enumerate([(wgT, s_wg16), (wuT, s_wu16)]):
                for c in range(8):
                    i = c % 2
                    em.dma("sync", wst[i][:], src[P * c:P * (c + 1), :], w=[f"wst{i}"])
                    em.op("scalar", lambda EE, i=i: EE.activation(wsb[i][:], wst[i][:], AF.Copy, scale=DEQ),
                          r=[f"wst{i}"], w=[f"wsb{i}"])
                    em.dma("gpsimd", dst[P * c:P * (c + 1), :], wsb[i][:],
                           r=[f"wsb{i}"], w=["s_w16"])
            for c in range(24):
                i = c % 2
                em.dma("sync", wst[i][:, 0:E], wdT[P * c:P * (c + 1), :], w=[f"wst{i}"])
                em.op("scalar", lambda EE, i=i: EE.activation(wsb[i][:, 0:E], wst[i][:, 0:E], AF.Copy, scale=DEQ),
                      r=[f"wst{i}"], w=[f"wsb{i}"])
                em.dma("gpsimd", s_wd16[P * c:P * (c + 1), :], wsb[i][:, 0:E],
                       r=[f"wsb{i}"], w=["s_w16"])

            y1l = [SB(ph6w, f"y1l{i}", [P, E]) for i in range(2)]
            y1q = [SB(ph6w, f"y1q{i}", [P, E]) for i in range(2)]
            lno = [SB(ph6w, f"lno{i}", [P, E], BF16) for i in range(2)]
            st6 = SB(ph6w, "st6", [P, 8])
            for t in range(16):
                i = t % 2
                em.dma("sync", y1l[i][:], s_y1own[P * t:P * (t + 1), :], r=["s_y1own"], w=[f"y1l{i}"])
                em.op("vector", lambda EE, i=i: EE.reduce_sum(st6[:, 0:1], y1l[i][:], AX.X),
                      r=[f"y1l{i}"], w=["sc0"])
                em.op("scalar", lambda EE, i=i: EE.activation(y1q[i][:], y1l[i][:], AF.Square),
                      r=[f"y1l{i}"], w=[f"y1q{i}"])
                em.op("vector", lambda EE, i=i: EE.reduce_sum(st6[:, 1:2], y1q[i][:], AX.X),
                      r=[f"y1q{i}"], w=["sc1"])
                em.op("vector", lambda EE: EE.tensor_scalar(st6[:, 2:3], st6[:, 0:1], 1.0 / E, None, ALU.mult),
                      r=["sc0"], w=["sc2"])
                em.op("vector", lambda EE: EE.tensor_scalar(st6[:, 3:4], st6[:, 1:2], 1.0 / E, None, ALU.mult),
                      r=["sc1"], w=["sc3"])
                em.op("vector", lambda EE: EE.tensor_tensor(st6[:, 4:5], st6[:, 2:3], st6[:, 2:3], ALU.mult),
                      r=["sc2"], w=["sc4"])
                em.op("vector", lambda EE: EE.tensor_tensor(st6[:, 5:6], st6[:, 3:4], st6[:, 4:5], ALU.subtract),
                      r=["sc3", "sc4"], w=["sc5"])
                em.op("scalar", lambda EE: EE.activation(st6[:, 6:7], st6[:, 5:6], AF.Sqrt, bias=epsb_s[:, 0:1]),
                      r=["sc5", "epsb_s"], w=["sc6"])
                em.op("vector", lambda EE: EE.reciprocal(st6[:, 7:8], st6[:, 6:7]),
                      r=["sc6"], w=["sc7"])
                em.op("vector", lambda EE, i=i: EE.tensor_scalar(
                    lno[i][:], y1l[i][:], st6[:, 2:3], st6[:, 7:8], ALU.subtract, ALU.mult),
                    r=[f"y1l{i}", "sc2", "sc7"], w=[f"lno{i}"])
                em.dma("gpsimd", s_ln2[P * t:P * (t + 1), :], lno[i][:],
                       r=[f"lno{i}"], w=["s_ln2"])
            em.barrier()

        with contextlib.ExitStack() as ph6:
            ln2T = [SB(ph6, f"ln2T{c}", [P, IN], BF16, is_transpose_dest=True) for c in range(8)]
            for c in range(8):
                em.dma_op("sync", lambda EE, c=c: EE.dma_start_transpose(
                    ln2T[c][:], s_ln2[:, P * c:P * (c + 1)]), r=["s_ln2"], w=[f"ln2T{c}"])
            wd16 = [SB(ph6, f"wd16_{c}", [P, E], BF16) for c in range(24)]
            for c in range(24):
                em.dma("sync", wd16[c][:], s_wd16[P * c:P * (c + 1), :], r=["s_w16"], w=[f"wd16_{c}"])
            wgs = [SB(ph6, f"wgs{i}", [P, 8 * P], BF16) for i in range(2)]
            wus = [SB(ph6, f"wus{i}", [P, 8 * P], BF16) for i in range(2)]
            hidT = [SB(ph6, f"hidT{c}", [P, 512], BF16) for c in range(24)]
            sg_b = SB(ph6, "sg_b", [P, 512], BF16)
            fo = [SB(ph6, f"fo{i}", [P, E]) for i in range(2)]
            fq = [SB(ph6, f"fq{i}", [P, E]) for i in range(2)]
            yo = [SB(ph6, f"yo{i}", [P, E]) for i in range(2)]
            y1r = [SB(ph6, f"y1r{i}", [P, E]) for i in range(2)]
            st7 = SB(ph6, "st7", [P, 8])
            ps_gt = PS(ph6, "ps_gt", [P, 512])
            ps_ut = PS(ph6, "ps_ut", [P, 512])
            ps_f = [PS(ph6, f"ps_f{i}", [P, 512]) for i in range(4)]

            for ti in range(4):
                for hh in range(24):
                    i = hh % 2
                    em.dma("sync", wgs[i][:].rearrange("p (k q) -> p k q", k=8),
                           s_wg16.rearrange("(k p) q -> p k q", p=P)[:, :, P * hh:P * (hh + 1)],
                           r=["s_w16"], w=[f"wgs{i}"])
                    em.dma("sync", wus[i][:].rearrange("p (k q) -> p k q", k=8),
                           s_wu16.rearrange("(k p) q -> p k q", p=P)[:, :, P * hh:P * (hh + 1)],
                           r=["s_w16"], w=[f"wus{i}"])
                    for k in range(8):
                        em.op("tensor", lambda EE, hh=hh, k=k, ti=ti, i=i: EE.matmul(
                            ps_gt[:], wgs[i][:, P * k:P * (k + 1)], ln2T[k][:, 512 * ti:512 * (ti + 1)],
                            start=(k == 0), stop=(k == 7)),
                            r=[f"wgs{i}", f"ln2T{k}"], w=["ps_gt"])
                        em.op("tensor", lambda EE, hh=hh, k=k, ti=ti, i=i: EE.matmul(
                            ps_ut[:], wus[i][:, P * k:P * (k + 1)], ln2T[k][:, 512 * ti:512 * (ti + 1)],
                            start=(k == 0), stop=(k == 7)),
                            r=[f"wus{i}", f"ln2T{k}"], w=["ps_ut"])
                    em.op("scalar", lambda EE: EE.activation(sg_b[:], ps_gt[:], AF.Silu),
                          r=["ps_gt"], w=["sg_b"])
                    em.op("vector", lambda EE, hh=hh: EE.tensor_tensor(
                        hidT[hh][:], ps_ut[:], sg_b[:], ALU.mult),
                        r=["ps_ut", "sg_b"], w=[f"hidT{hh}"])
                for tt in range(4):
                    i = tt % 2
                    for hh in range(24):
                        em.op("tensor", lambda EE, hh=hh, tt=tt, i=i: EE.matmul(
                            ps_f[2 * i][:], hidT[hh][:, P * tt:P * (tt + 1)], wd16[hh][:, 0:512],
                            start=(hh == 0), stop=(hh == 23)),
                            r=[f"hidT{hh}", f"wd16_{hh}"], w=[f"ps_f{2 * i}"])
                        em.op("tensor", lambda EE, hh=hh, tt=tt, i=i: EE.matmul(
                            ps_f[2 * i + 1][:], hidT[hh][:, P * tt:P * (tt + 1)], wd16[hh][:, 512:1024],
                            start=(hh == 0), stop=(hh == 23)),
                            r=[f"hidT{hh}", f"wd16_{hh}"], w=[f"ps_f{2 * i + 1}"])
                    em.op("vector", lambda EE, i=i: EE.tensor_copy(fo[i][:, 0:512], ps_f[2 * i][:]),
                          r=[f"ps_f{2 * i}"], w=[f"fo{i}"])
                    em.op("vector", lambda EE, i=i: EE.tensor_copy(fo[i][:, 512:1024], ps_f[2 * i + 1][:]),
                          r=[f"ps_f{2 * i + 1}"], w=[f"fo{i}"])
                    row = P * (4 * ti + tt)
                    em.dma("sync", y1r[i][:], s_y1own[row:row + P, :], r=["s_y1own"], w=[f"y1r{i}"])
                    em.op("vector", lambda EE, i=i: EE.reduce_sum(st7[:, 0:1], fo[i][:], AX.X),
                          r=[f"fo{i}"], w=["sd0"])
                    em.op("scalar", lambda EE, i=i: EE.activation(fq[i][:], fo[i][:], AF.Square),
                          r=[f"fo{i}"], w=[f"fq{i}"])
                    em.op("vector", lambda EE, i=i: EE.reduce_sum(st7[:, 1:2], fq[i][:], AX.X),
                          r=[f"fq{i}"], w=["sd1"])
                    em.op("vector", lambda EE: EE.tensor_scalar(st7[:, 2:3], st7[:, 0:1], 1.0 / E, None, ALU.mult),
                          r=["sd0"], w=["sd2"])
                    em.op("vector", lambda EE: EE.tensor_scalar(st7[:, 3:4], st7[:, 1:2], 1.0 / E, None, ALU.mult),
                          r=["sd1"], w=["sd3"])
                    em.op("vector", lambda EE: EE.tensor_tensor(st7[:, 4:5], st7[:, 2:3], st7[:, 2:3], ALU.mult),
                          r=["sd2"], w=["sd4"])
                    em.op("vector", lambda EE: EE.tensor_tensor(st7[:, 5:6], st7[:, 3:4], st7[:, 4:5], ALU.subtract),
                          r=["sd3", "sd4"], w=["sd5"])
                    em.op("scalar", lambda EE: EE.activation(st7[:, 6:7], st7[:, 5:6], AF.Sqrt, bias=epsb_s[:, 0:1]),
                          r=["sd5", "epsb_s"], w=["sd6"])
                    em.op("vector", lambda EE: EE.reciprocal(st7[:, 7:8], st7[:, 6:7]),
                          r=["sd6"], w=["sd7"])
                    em.op("vector", lambda EE, i=i: EE.tensor_scalar(
                        fo[i][:], fo[i][:], st7[:, 2:3], st7[:, 7:8], ALU.subtract, ALU.mult),
                        r=["sd2", "sd7"], w=[f"fo{i}"])
                    em.op("vector", lambda EE, i=i: EE.tensor_tensor(
                        yo[i][:], fo[i][:], y1r[i][:], ALU.add),
                        r=[f"fo{i}", f"y1r{i}"], w=[f"yo{i}"])
                    em.dma("gpsimd", y[row:row + P, :], yo[i][:], r=[f"yo{i}"], w=["y_out"])

            fence_waits = [(f"q_gpsimd{i}", em.cnt[f"q_gpsimd{i}"]) for i in range(NQ)]
            fence_waits += [(f"q_sync{i}", em.cnt[f"q_sync{i}"]) for i in range(NQ)]
            em.op("tensor", lambda EE: EE.drain(), extra=fence_waits)

        with nc.Block() as block:
            em.replay(block)
    return nc


def stage_inputs(inputs):
    g = lambda k: np.asarray(inputs[k])
    x_full = g("L0_attn_x").astype(np.float32)
    wup = np.ascontiguousarray(g("L0_attn_W_up"))
    conv_w = g("L0_attn_conv_w")
    conv_b = g("L0_attn_conv_b")
    Wq = g("L0_attn_Wq"); Wk = g("L0_attn_Wk"); Wv = g("L0_attn_Wv")
    W_ig = g("L0_attn_W_ig"); W_fg = g("L0_attn_W_fg")
    b_ig = g("L0_attn_b_ig"); b_fg = g("L0_attn_b_fg")
    norm_w = g("L0_attn_norm_w"); skip = g("L0_attn_skip")
    wdn = np.ascontiguousarray(g("L0_attn_W_down"))
    wg = g("L0_ffn_w_gate"); wu = g("L0_ffn_w_up"); wd = g("L0_ffn_w_down")

    convdiag = np.zeros((16, KCV, P, P), np.float32)
    for c in range(16):
        for i in range(KCV):
            np.fill_diagonal(convdiag[c, i], conv_w[i, P * c:P * (c + 1)])
    convb2d = np.ascontiguousarray(conv_b.reshape(16, P).T).astype(np.float32)

    def blockdiag(W, transposed):
        out = np.zeros((16, P, P), np.float32)
        for c in range(16):
            for gidx in range(32):
                blk = W[32 * c + gidx]
                if transposed:
                    out[c, 4 * gidx:4 * gidx + 4, 4 * gidx:4 * gidx + 4] = blk
                else:
                    out[c, 4 * gidx:4 * gidx + 4, 4 * gidx:4 * gidx + 4] = blk.T
        return out

    bdq_ = blockdiag(Wq, False); bdk_ = blockdiag(Wk, False); bdv_ = blockdiag(Wv, False)
    bdqt_ = blockdiag(Wq, True); bdkt_ = blockdiag(Wk, True); bdvt_ = blockdiag(Wv, True)

    wigfg = np.zeros((3 * IN, 8), np.float32)
    wigfg[:, 0:4] = W_ig
    wigfg[:, 4:8] = W_fg
    bigfg = np.concatenate([b_ig, b_fg]).reshape(8, 1).astype(np.float32)
    nw2d = np.ascontiguousarray(norm_w.reshape(16, P).T).astype(np.float32)
    skip2d = np.ascontiguousarray(skip.reshape(16, P).T).astype(np.float32)
    nwrow = norm_w.reshape(1, IN).astype(np.float32)

    masks_h = np.zeros((4, P, 512), np.float32)
    pp = np.arange(P)[:, None]
    nn = np.arange(512)[None, :]
    for r in range(4):
        masks_h[r] = 30000.0 * (pp > nn - P * r)

    common = dict(
        wup=wup.astype(np.float32),
        convdiag=convdiag, convb2d=convb2d,
        bdq=bdq_, bdk=bdk_, bdv=bdv_, bdqt=bdqt_, bdkt=bdkt_, bdvt=bdvt_,
        wigfg=wigfg, bigfg=bigfg, nw2d=nw2d, nwrow=nwrow, skip2d=skip2d,
        wdn=wdn.astype(np.float32),
        wgT=np.ascontiguousarray(wg.T).astype(np.int32),
        wuT=np.ascontiguousarray(wu.T).astype(np.int32),
        wdT=np.ascontiguousarray(wd.T).astype(np.int32),
        masks=masks_h,
        ident=np.eye(P, dtype=np.float32),
        onesrow=np.ones((1, P), np.float32),
        onescol=np.ones((P, 1), np.float32),
        epsb=np.tile(np.array([[1e-5, 1e-6, 1.0, 0.0]], np.float32), (P, 1)),
    )
    in_maps = []
    for c in range(8):
        b = c // 2
        m = dict(common)
        m["x"] = np.ascontiguousarray(x_full[b])
        in_maps.append(m)
    return in_maps


def kernel(**inputs):
    in_maps = stage_inputs(inputs)
    nc = build_program()
    res = run_bass_kernel_spmd(nc, in_maps, core_ids=list(range(8)))
    out = np.empty((4, S, E), np.float32)
    for b in range(4):
        out[b, 0:2048] = res.results[2 * b]["y"]
        out[b, 2048:4096] = res.results[2 * b + 1]["y"]
    return out
